# version 05
# baseline (speedup 1.0000x reference)
import math
import numpy as np
import ml_dtypes
import concourse.bass as bass
import concourse.mybir as mybir
from concourse.bass_utils import run_bass_kernel_spmd

F32 = mybir.dt.float32
BF16 = mybir.dt.bfloat16
FP8 = mybir.dt.float8e4
AF = mybir.ActivationFunctionType
OP = mybir.AluOpType
AX = mybir.AxisListType.X

NCORE = 8
TOK = 1024
D = 2048
DC = D // 128
NT = TOK // 128
MAGIC = float(3 * 2.0 ** 22)
ISQD = float(1.0 / math.sqrt(2048.0))

ENGINES = ("sync", "tensor", "vector", "scalar", "gpsimd")
_TAPS = ()
_NPHASE = 99


def _norm_waits(after):
    m = {}
    for t in after:
        if t is None:
            continue
        s, v = t
        if v > m.get(s, 0):
            m[s] = v
    return tuple(m.items())


QUEUE_SEMS = {"sync": [f"dS{i}" for i in range(6)],
              "scalar": [f"dA{i}" for i in range(4)],
              "gpsimd": [f"dG{i}" for i in range(6)]}


class B:

    def __init__(self):
        self.q = {e: [] for e in ENGINES}
        self.cnt = {}
        self.qrot = {e: 0 for e in QUEUE_SEMS}
        self.serial = False

    def _bump(self, sem, n):
        self.cnt[sem] = self.cnt.get(sem, 0) + n
        return (sem, self.cnt[sem])

    def c(self, eng, fn, after=(), sem=None, n=1):
        if self.serial:
            after = list(after) + [(s, v) for s, v in self.cnt.items()]
        tok = self._bump(sem, n) if sem else None
        self.q[eng].append(("c", _norm_waits(after), fn, sem, n))
        return tok

    def d(self, eng, fn, ndma=1, after=(), sem=None):
        if self.serial:
            after = list(after) + [(s, v) for s, v in self.cnt.items()]
        sems = QUEUE_SEMS[eng]
        i = self.qrot[eng]
        self.qrot[eng] = (i + 1) % len(sems)
        sem = sems[i]
        gate = (sem, self.cnt.get(sem, 0))
        tok = self._bump(sem, 16 * ndma)
        self.q[eng].append(("d", _norm_waits(list(after) + [gate]), fn, sem, ndma))
        return tok


class Bank:
    def __init__(self, ap):
        self.ap = ap
        self.wtok = None
        self.rtok = None


def _check_schedule(b):
    pos = {e: 0 for e in ENGINES}
    cnt = {s: 0 for s in b.cnt}
    progress = True
    while progress:
        progress = False
        for e in ENGINES:
            while pos[e] < len(b.q[e]):
                kind, waits, fn, sem, n = b.q[e][pos[e]]
                if any(cnt[s] < v for s, v in waits):
                    break
                if sem is not None:
                    cnt[sem] += (16 * n) if kind == "d" else n
                pos[e] += 1
                progress = True
    stuck = {e: (pos[e], len(b.q[e])) for e in ENGINES if pos[e] < len(b.q[e])}
    if stuck:
        detail = []
        for e, (p, tot) in stuck.items():
            kind, waits, fn, sem, n = b.q[e][p]
            blocked = [(s, v, cnt[s]) for s, v in waits if cnt[s] < v]
            detail.append(f"{e}[{p}/{tot}] blocked on {blocked}")
        raise RuntimeError("schedule deadlock:\n" + "\n".join(detail))


def _emit(nc, b):
    _check_schedule(b)
    from contextlib import ExitStack
    with ExitStack() as es:
        sems = {name: es.enter_context(nc.semaphore(name)) for name in sorted(b.cnt)}

        def mk(eng_name):
            def run(h):
                last_wait = {}
                for kind, waits, fn, sem, n in b.q[eng_name]:
                    for s, v in waits:
                        if last_wait.get(s, -1) >= v:
                            continue
                        h.wait_ge(sems[s], v)
                        last_wait[s] = v
                    if kind == "c":
                        last = fn(h)
                        if sem is not None:
                            assert last is not None
                            last.then_inc(sems[sem], n)
                    else:
                        instrs = fn(h)
                        assert len(instrs) == n, (eng_name, len(instrs), n)
                        for ins in instrs:
                            ins.then_inc(sems[sem], 16)
            return run

        runners = {e: mk(e) for e in ENGINES}
        with nc.Block() as block:
            @block.sync
            def _(h):
                runners["sync"](h)

            @block.tensor
            def _(h):
                runners["tensor"](h)

            @block.vector
            def _(h):
                runners["vector"](h)

            @block.scalar
            def _(h):
                runners["scalar"](h)

            @block.gpsimd
            def _(h):
                runners["gpsimd"](h)


def build_nc(nphase=None):
    if nphase is None:
        nphase = _NPHASE
    nc = bass.Bass("TRN2", target_bir_lowering=False, debug=False)
    b = B()

    def inp(name, shape, dt=F32):
        return nc.dram_tensor(name, list(shape), dt, kind="ExternalInput").ap()

    x_in = inp("x", (TOK, D))
    WT = {w: inp(f"W{w}T", (D, D)) for w in "ifgo"}
    nrm = {w: inp(f"nrm_{w}", (1, D)) for w in "ifgo"}
    gnw_in = inp("gnw", (1, D))
    W8T = inp("W8T", (D, D), FP8)
    w8s_in = inp("w8s", (1, 1))
    ins_in = inp("in_s", (1, 1), BF16)
    c01 = inp("c01", (1, 2 * D))
    wmod = [inp("wmod0", (D, 1536)), inp("wmod1", (D, 1536))]
    WqkvT = inp("WqkvT", (D, 3 * D))
    bqkv = inp("bqkv", (1, 3 * D))
    WoutT = inp("WoutT", (D, D))
    bout = inp("bout", (1, D))
    gateT = inp("gateT", (D, 8))
    gWT = inp("gWT", (8, D, 1408))
    uWT = inp("uWT", (8, D, 1408))
    dWT = inp("dWT", (8, 1408, D))
    sgWT = inp("sgWT", (D, 2816))
    suWT = inp("suWT", (D, 2816))
    sdWT = inp("sdWT", (2816, D))
    id32_in = inp("id32", (128, 128))
    cvec_in = inp("cvec", (1, 8))
    idbf_in = inp("idbf", (128, 128), BF16)
    rmask_in = inp("rmask", (1, 8))

    y_out = nc.dram_tensor("y", [TOK, D], F32, kind="ExternalOutput").ap()

    def scr(name, shape, dt=F32):
        return nc.dram_tensor(name, list(shape), dt, kind="Internal").ap()

    cc_mod_in = scr("cc_mod_in", (1, 3072))
    cc_mod_out = scr("cc_mod_out", (8, 3072))
    modL = [scr("modL0", (1, 6 * D)), scr("modL1", (1, 6 * D))]
    cc_hg_in = scr("cc_hg_in", (1, 2 * D))
    cc_hg_out = scr("cc_hg_out", (8, 2 * D))
    scr_f = scr("scr_f", (D, TOK))
    scr_i = scr("scr_i", (D, TOK))
    scr_g = scr("scr_g", (TOK, D))
    scr_h = scr("scr_h", (TOK, D))
    xres = scr("xres", (TOK, D))
    srec_d = scr("srec_d", (4, TOK))
    smallv = scr("smallv", (1, 64))
    KVC1 = D * TOK + 32
    cc_k_in = scr("cc_k_in", (1, KVC1), BF16)
    cc_k_out = scr("cc_k_out", (8, KVC1), BF16)
    ccv_i = scr("ccv_i", (1, TOK * D), BF16)
    ccv_o = scr("ccv_o", (8, TOK * D), BF16)
    qn_d = scr("qn_d", (1, TOK))
    dbg1 = scr("dbg1", (TOK, D))
    dbg2 = scr("dbg2", (TOK, D))
    dbg3 = scr("dbg3", (128, DC * TOK), BF16)
    dbg4 = scr("dbg4", (128, D))
    dbg5 = scr("dbg5", (1, 64))
    dbg6 = scr("dbg6", (D, TOK))
    dbg7 = scr("dbg7", (D, TOK))

    taps = {}
    for name in _TAPS:
        src = {"scr_f": scr_f, "scr_i": scr_i, "scr_g": scr_g, "scr_h": scr_h,
               "xres": xres, "modL0": modL[0], "modL1": modL[1],
               "srec_d": srec_d, "qn_d": qn_d, "dbg1": dbg1, "dbg2": dbg2,
               "dbg3": dbg3, "dbg4": dbg4, "dbg5": dbg5, "dbg6": dbg6,
               "dbg7": dbg7}[name]
        taps[name] = (src, nc.dram_tensor("tap_" + name, list(src.tensor.shape),
                                          src.dtype, kind="ExternalOutput").ap())

    KB = 1024
    arena_h = nc.alloc_sbuf_tensor("arena", [128, 178 * KB], mybir.dt.uint8)
    base = nc.lookup_mloc(arena_h).addr

    def at(name, off, shape, dt):
        return nc.alloc_sbuf_tensor_at(name, list(shape), dt, offset=base + off).ap()

    OFF_A, OFF_B, OFF_C, OFF_P = 0, 32 * KB, 64 * KB, 96 * KB
    OFF_T0, OFF_T1 = 136 * KB, 144 * KB
    OFF_W0, OFF_W1 = 152 * KB, 160 * KB
    OFF_S = 168 * KB

    kq_i = at("kq_i", OFF_A, (128, DC * TOK), BF16)
    kq_f = at("kq_f", OFF_B, (128, DC * TOK), BF16)
    kq_g = at("kq_g", OFF_A, (128, DC * TOK), BF16)
    kq_o = at("kq_o", OFF_A, (128, DC * TOK), BF16)
    q8T = at("q8T", OFF_A, (128, DC * TOK), BF16)
    wmodb = [at(f"wmodb{i}", OFF_A + i * 6 * KB, (128, 1536), F32) for i in range(2)]
    sreb = [at(f"sreb{i}", OFF_C + i * 4 * KB, (128, TOK), F32) for i in range(3)]
    scanio = at("scanio", OFF_C + 8 * KB, (128, 2 * TOK), F32)
    scano = at("scano", OFF_C + 16 * KB, (128, TOK), F32)
    ev = [at(f"ev{i}", OFF_C + 20 * KB + i * 4 * KB, (128, TOK), F32) for i in range(3)]
    sreb_g = at("sreb_g", OFF_B, (128, TOK), F32)
    ev2 = [at(f"ev2{i}", OFF_B + 4 * KB + i * 4 * KB, (128, TOK), F32) for i in range(2)]
    gtile = at("gtile", OFF_B + 12 * KB, (128, D), F32)
    htile = at("htile", OFF_B + 20 * KB, (128, D), F32)
    Qfm = at("Qfm", OFF_A, (128, DC * TOK), BF16)
    hfm = at("hfm", OFF_B, (128, DC * TOK), BF16)
    ctx = [at(f"ctx{t}", (OFF_B if t < 4 else OFF_C) + (t % 4) * 8 * KB, (128, D), F32)
           for t in range(NT)]
    acc = [at(f"acc{t}", (OFF_A if t < 4 else OFF_B) + (t % 4) * 8 * KB, (128, D), F32)
           for t in range(NT)]
    h3fm = at("h3fm", OFF_C, (128, DC * TOK), BF16)

    bc = [at(f"bc{i}", OFF_P + i * 8 * KB, (128, D), F32) for i in range(4)]
    mq = [at(f"mq{i}", OFF_P + 32 * KB + i * 4 * KB, (128, DC * 128), BF16)
          for i in range(2)]
    kbuf = [at(f"kbuf{i}", OFF_P + i * 8 * KB, (128, DC * 256), BF16) for i in range(2)]
    vbuf = [at(f"vbuf{i}", OFF_P + 16 * KB + i * 8 * KB, (128, 2 * D), BF16)
            for i in range(2)]
    ptq = [at(f"ptq{i}", OFF_P + 32 * KB + i * 2 * KB, (128, 8 * 128), BF16)
           for i in range(2)]
    pP = [at(f"pP{i}", OFF_P + 36 * KB + i * KB, (128, 512), BF16) for i in range(2)]
    wobuf = [at(f"wobuf{i}", OFF_P + 38 * KB + i * KB, (128, 512), BF16)
             for i in range(2)]
    gu_e = at("gu_e", OFF_P, (128, 11 * TOK), BF16)
    gwb = [at(f"gwb{i}", OFF_P + 22 * KB + i * 4 * KB, (128, DC * 128), BF16)
           for i in range(2)]
    uwb = [at(f"uwb{i}", OFF_P + 30 * KB + i * 4 * KB, (128, DC * 128), BF16)
           for i in range(2)]
    dwb = [at(f"dwb{i}", OFF_P + 38 * KB + i * KB, (128, 512), BF16) for i in range(2)]
    w8b = [at(f"w8b{i}", OFF_P + 38 * KB + i * KB, (128, 512), BF16) for i in range(2)]
    tm0 = at("tm0", OFF_T0, (128, D), F32)
    tm1 = at("tm1", OFF_T1, (128, D), F32)
    wst = [at(f"wst{i}", OFF_W0 if i == 0 else OFF_W1, (128, D), F32) for i in range(2)]

    sm_off = [OFF_S]

    def small(name, shape, dt=F32):
        sz = int(np.prod(shape[1:])) * {F32: 4, BF16: 2, FP8: 1}[dt]
        sz = (sz + 31) // 32 * 32
        ap = at(name, sm_off[0], shape, dt)
        sm_off[0] += sz
        return ap

    id32 = small("id32s", (128, 128))
    statw = small("statws", (128, 64))
    idbf = small("idbfs", (128, 128), BF16)
    silc = small("silcs", (128, 2 * DC))
    mod_sb = at("mod_sbs", OFF_B, (1, 3072), F32)
    stat = small("stats", (128, 32))
    swv = small("swvs", (128, 8))
    carry = small("carrys", (128, DC))
    facc = small("faccs", (128, DC))
    hcar = small("hcars", (128, DC))
    lnf = small("lnfs", (128, DC))
    rmk = small("rmks", (128, 8))
    ghF = small("ghFs", (128, DC))
    ghH = small("ghHs", (128, DC))
    gtmp = small("gtmps", (128, DC))
    wab = small("wabs", (128, 8))
    onf = small("onfs", (128, 1))
    sc1 = small("sc1s", (1, 64))
    srow = small("srows", (1, 128))
    c128 = small("c128s", (128, 8))
    qn_sb = small("qn_sbs", (128, NT))
    biasB = small("biasBs", (128, NT))
    lpart = small("lparts", (128, 32))
    lrec = small("lrecs", (128, NT))
    gateTs = small("gateTss", (128, DC * 8))
    wmat_sb = small("wmat_sbs", (128, NT * 8))
    hfm32 = small("hfm32s", (128, 128))
    scn2 = small("scn2s", (128, 256))
    scn2o = small("scn2os", (128, 128))
    bq_sb = small("bq_sbs", (128, 48))
    cvec = small("cvecs", (128, 8))
    rt = small("rts", (128, 16))
    rt2 = small("rt2s", (128, 16))
    oinv = small("oinvs", (128, NT))
    assert sm_off[0] <= OFF_S + 10 * KB, sm_off[0]

    pb = [Bank(nc.alloc_psum_tensor(f"pb{i}", [128, 512], F32).ap()) for i in range(6)]
    pt = [Bank(nc.alloc_psum_tensor(f"pt{i}", [128, 1024], BF16, side="right").ap())
          for i in range(2)]

    def dma(eng, out_ap, in_ap, after=(), slow=False):
        return b.d(eng, lambda h, o=out_ap, i=in_ap, sl=slow: [
            h.dma_start(out=o, in_=i, allow_slow_non_contiguous=sl)],
            1, after=after)

    def bcast(eng, dst_sb, src_row, after=()):
        n = dst_sb.shape[-1]
        return dma(eng, dst_sb, src_row.broadcast_to((128, n)), after=after)

    def vop(fn, after=()):
        return b.c("vector", fn, after=after, sem="tV")

    def sop(fn, after=()):
        return b.c("scalar", fn, after=after, sem="tS")

    def top(fn, after=()):
        return b.c("tensor", fn, after=after, sem="tT")

    def gop(fn, after=()):
        return b.c("gpsimd", fn, after=after, sem="cc")

    def ag(cin, cout, after):
        return gop(lambda h: h.collective_compute(
            "AllGather", OP.bypass, replica_groups=[list(range(8))],
            ins=[cin.tensor.ap()], outs=[cout.tensor.ap()]), after=after)

    t_id32 = dma("scalar", id32, id32_in)
    t_cvec = bcast("scalar", cvec, cvec_in[0:1, :])
    t_idbf = dma("scalar", idbf, idbf_in)
    t_rmk = bcast("scalar", rmk, rmask_in[0:1, :])
    t_gateT = dma("scalar", gateTs.rearrange("p (c e) -> p c e", e=8),
                  gateT.rearrange("(c p) e -> p c e", p=128))
    t_bq = dma("scalar", bq_sb, bqkv[0, :].rearrange("(j p) -> p j", p=128),
               slow=True)
    t_c = dma("scalar", silc, c01[0, :].rearrange("(j p) -> p j", p=128), slow=True)
    t_sil = sop(lambda h: h.activation(silc, silc, AF.Silu), after=[t_c])

    wm_ld = {}
    wm_mm = {}
    for L in range(2):
        for k in range(DC):
            prevmm = wm_mm.get((L, k - 2)) or (wm_mm.get((L - 1, DC - 2 + k))
                                               if L and k < 2 else None)
            wm_ld[(L, k)] = dma("sync", wmodb[k % 2], wmod[L][k * 128:(k + 1) * 128, :],
                                after=[prevmm] if prevmm else [])

            def mkmm(L=L, k=k):
                def f(h):
                    last = None
                    for n in range(3):
                        last = h.matmul(pb[3 * L + n].ap[0:1, :],
                                        silc[:, L * DC + k:L * DC + k + 1],
                                        wmodb[k % 2][:, n * 512:(n + 1) * 512],
                                        start=(k == 0), stop=(k == DC - 1))
                    return last
                return f
            wm_mm[(L, k)] = top(mkmm(), after=[wm_ld[(L, k)], t_sil,
                                               wm_mm.get((L, k - 1))])
    mod_ev = []
    for L in range(2):
        for n in range(3):
            bk = pb[3 * L + n]
            bk.wtok = wm_mm[(L, DC - 1)]
            off = L * 1536 + n * 512
            bk.rtok = vop(lambda h, bk=bk, off=off: h.tensor_copy(
                mod_sb[0:1, off:off + 512], bk.ap[0:1, :]), after=[bk.wtok])
            mod_ev.append(bk.rtok)
    t_modst = dma("sync", cc_mod_in[0:1, :], mod_sb[0:1, :], after=mod_ev)
    t_ag_mod = ag(cc_mod_in, cc_mod_out, [t_modst])
    t_mod = [dma("sync", modL[L][0, :].rearrange("(r n) -> r n", n=1536),
                 cc_mod_out[:, L * 1536:(L + 1) * 1536], after=[t_ag_mod])
             for L in range(2)]

    wz = vop(lambda h: h.memset(wab, 0.0))
    red_prev = {0: wz, 1: wz}
    last_red = wz
    wst_last_read = {0: None, 1: None}
    for wi, wname in enumerate("ifgo"):
        for k in range(DC):
            bi = k % 2
            tl = dma("sync", wst[bi], WT[wname][k * 128:(k + 1) * 128, :],
                     after=[wst_last_read[bi]])
            r1 = vop(lambda h, bi=bi: h.reduce_sum(
                stat[:, 0:1], wst[bi], axis=AX, apply_absolute_value=True),
                after=[tl, last_red])
            last_red = vop(lambda h, wi=wi: h.tensor_add(
                wab[:, wi:wi + 1], wab[:, wi:wi + 1], stat[:, 0:1]), after=[r1])
            wst_last_read[bi] = r1
    t_onf = vop(lambda h: h.memset(onf, 1.0))
    pb[0].wtok = top(lambda h: h.matmul(pb[0].ap[0:1, 0:4], onf, wab[:, 0:4],
                                        start=True, stop=True),
                     after=[last_red, t_onf, pb[0].rtok])
    t_mc = vop(lambda h: h.tensor_scalar(sc1[0:1, 0:4], pb[0].ap[0:1, 0:4],
                                         float(1.0 / (D * D)), 1e-5, OP.mult, OP.max),
               after=[pb[0].wtok])
    pb[0].rtok = t_mc
    t_mcr = vop(lambda h: h.reciprocal(sc1[0:1, 4:8], sc1[0:1, 0:4]), after=[t_mc])
    t_swst = dma("sync", smallv[0:1, 0:8], sc1[0:1, 0:8], after=[t_mcr])
    t_swv = dma("scalar", swv, smallv[0:1, 0:8].broadcast_to((128, 8)),
                after=[t_swst])

    if nphase < 1:
        return _finalize(nc, b, taps, y_out)

    tm0_free = [None]
    tm1_free = [None]
    srow_free = [None]
    statg = {0: [], 1: []}

    def ln_tile(x_src, t, bca, bcs, extra=()):
        cb = 32 * (t % 2)
        st = lambda c: statw[:, cb + c:cb + c + 1]
        guard = list(statg[t % 2])
        statg[t % 2] = []
        tx = dma("sync", tm0, x_src[t * 128:(t + 1) * 128, :],
                 after=[tm0_free[0]] + list(extra))
        t1 = vop(lambda h: h.reduce_sum(st(1), tm0, axis=AX), after=[tx] + guard)
        t2 = sop(lambda h: h.activation(tm1, tm0, AF.Square),
                 after=[tx] + tm1_free)
        t3 = vop(lambda h: h.reduce_sum(st(2), tm1, axis=AX), after=[t2])
        t4 = vop(lambda h: h.tensor_scalar_mul(st(1), st(1),
                                               float(1.0 / D)), after=[t1])
        t5 = vop(lambda h: h.tensor_mul(st(3), st(1), st(1)), after=[t4])
        t6 = vop(lambda h: h.tensor_scalar_mul(st(2), st(2),
                                               float(1.0 / D)), after=[t3])
        t7 = vop(lambda h: h.tensor_sub(st(2), st(2), st(3)), after=[t5, t6])
        t8 = sop(lambda h: h.activation(st(4), st(2), AF.Sqrt,
                                        bias=cvec[:, 0:1]), after=[t7, t_cvec])
        t9 = vop(lambda h: h.reciprocal(st(5), st(4)), after=[t8])
        ta = vop(lambda h: h.tensor_scalar(tm1, tm0, st(1), st(5),
                                           OP.subtract, OP.mult), after=[t9, t3])
        tb = vop(lambda h: h.tensor_tensor(tm1, tm1, bca, OP.mult), after=[ta])
        tc = vop(lambda h: h.tensor_tensor(tm1, tm1, bcs, OP.add), after=[tb])
        tm0_free[0] = ta
        statg[t % 2].append(t8)
        return tc

    def quant_tile(src_sb, t, nrm_bc, kq_dst, invs_col, srec_row, after,
                   eps=1e-8, keep_invs=None, region=0):
        cb = 32 * (t % 2) + 6 + 10 * region
        st = lambda c: statw[:, cb + c:cb + c + 1]
        guard = list(statg[t % 2])
        t2 = sop(lambda h: h.activation(tm0, src_sb, AF.Square),
                 after=list(after) + [tm0_free[0]])
        t3 = vop(lambda h: h.reduce_sum(st(0), tm0, axis=AX), after=[t2] + guard)
        epscol = 0 if eps == 1e-5 else 1
        t8 = sop(lambda h, epscol=epscol: h.activation(
            st(1), st(0), AF.Sqrt,
            bias=cvec[:, epscol:epscol + 1], scale=float(1.0 / D)),
            after=[t3, t_cvec])
        t9 = vop(lambda h: h.reciprocal(st(2), st(1)), after=[t8])
        ta = vop(lambda h: h.scalar_tensor_tensor(tm0, src_sb, st(2), nrm_bc,
                                                  OP.mult, OP.mult), after=[t9, t3])
        tb = vop(lambda h: h.reduce_max(st(3), tm0, axis=AX,
                                        apply_absolute_value=True), after=[ta])
        tc = vop(lambda h: h.tensor_scalar_max(st(4), st(3), 1e-5),
                 after=[tb])
        td = vop(lambda h: h.tensor_scalar_mul(st(6), st(4), float(1.0 / 127.0)),
                 after=[tc])
        te = vop(lambda h: h.reciprocal(st(5), st(4)), after=[tc])
        tf = vop(lambda h: h.tensor_scalar_mul(st(5), st(5), 127.0),
                 after=[te])
        tg = vop(lambda h: h.tensor_scalar(tm0, tm0, st(5), MAGIC,
                                           OP.mult, OP.add), after=[tf])
        th = vop(lambda h: h.tensor_scalar(tm0, tm0, MAGIC, 127.0,
                                           OP.subtract, OP.min), after=[tg])
        ti = vop(lambda h: h.tensor_scalar_max(tm0, tm0, -128.0), after=[th])
        toks = []
        for half in range(4):
            bk = pb[4 + (half % 2)]
            def mkT(half=half, bk=bk):
                def f(h):
                    last = None
                    for j in range(4):
                        dc = half * 4 + j
                        last = h.transpose(bk.ap[:, j * 128:(j + 1) * 128],
                                           tm0[:, dc * 128:(dc + 1) * 128], id32)
                    return last
                return f
            bk.wtok = top(mkT(), after=[ti, bk.rtok, t_id32])
            def mkE(half=half, bk=bk):
                def f(h):
                    last = None
                    for j in range(4):
                        dc = half * 4 + j
                        last = h.tensor_copy(
                            kq_dst[:, dc * TOK + t * 128:dc * TOK + (t + 1) * 128],
                            bk.ap[:, j * 128:(j + 1) * 128])
                    return last
                return f
            bk.rtok = vop(mkE(), after=[bk.wtok])
            toks.append(bk.rtok)
        tm0_free[0] = toks[-1]
        statg[t % 2].append(t8)
        if keep_invs is not None:
            tk = vop(lambda h: h.tensor_scalar(
                oinv[:, t:t + 1], st(6),
                swv[:, keep_invs:keep_invs + 1], None, OP.mult),
                after=[td, t_swv])
            toks.append(tk)
        if srec_row is not None:
            bk = pb[3]
            bk.wtok = top(lambda h, bk=bk, st=st: h.transpose(
                bk.ap[0:1, 0:128], st(6), id32),
                after=[td, bk.rtok, t_id32])
            tse = vop(lambda h, bk=bk: h.tensor_copy(srow[0:1, 0:128],
                                                     bk.ap[0:1, 0:128]),
                      after=[bk.wtok, srow_free[0]])
            bk.rtok = tse
            tsd = dma("sync", srec_d[srec_row:srec_row + 1, t * 128:(t + 1) * 128],
                      srow[0:1, 0:128], after=[tse])
            srow_free[0] = tsd
            toks.append(tsd)
            statg[t % 2].append(bk.wtok)
        return toks

    def stream_quant_w(wname, och, buf_i, after):
        wi = "ifgo".index(wname)
        tl = dma("sync", wst[buf_i].rearrange("p (c q) -> p c q", q=128),
                 WT[wname][:, och * 128:(och + 1) * 128].rearrange(
                     "(c p) q -> p c q", p=128), after=after)
        q1 = vop(lambda h, bi=buf_i, wi=wi: h.tensor_scalar(
            wst[bi], wst[bi], swv[:, 4 + wi:5 + wi], MAGIC, OP.mult, OP.add),
            after=[tl, t_swv])
        q2 = vop(lambda h, bi=buf_i: h.tensor_scalar(
            wst[bi], wst[bi], MAGIC, 1.0, OP.subtract, OP.min), after=[q1])
        q3 = vop(lambda h, bi=buf_i: h.tensor_scalar_max(mq[bi], wst[bi], -1.0),
                 after=[q2])
        return q3

    def bit_mm(kq_src, mqb, bank, n, after):
        def f(h):
            last = None
            for dc in range(DC):
                last = h.matmul(bank.ap,
                                mqb[:, dc * 128:(dc + 1) * 128],
                                kq_src[:, dc * TOK + n * 512:dc * TOK + (n + 1) * 512],
                                start=(dc == 0), stop=(dc == DC - 1))
            return last
        return top(f, after=list(after) + [bank.rtok])

    t_bca = bcast("scalar", bc[0], modL[0][0:1, D:2 * D], after=[t_mod[0]])
    t_bca = vop(lambda h: h.tensor_scalar_add(bc[0], bc[0], 1.0), after=[t_bca])
    t_bcs = bcast("scalar", bc[1], modL[0][0:1, 0:D], after=[t_mod[0]])
    t_bni = bcast("scalar", bc[2], nrm["i"][0:1, :])
    t_bnf = bcast("scalar", bc[3], nrm["f"][0:1, :])

    sw1_done = []
    for t in range(NT):
        th = ln_tile(x_in, t, bc[0], bc[1], extra=[t_bca, t_bcs, t_bni, t_bnf])
        hst = dma("sync", scr_h[t * 128:(t + 1) * 128, :], tm1, after=[th])
        ks_i = quant_tile(tm1, t, bc[2], kq_i, 12, 0, after=[th])
        ks_f = quant_tile(tm1, t, bc[3], kq_f, 13, 1, after=[th], region=1)
        tm1_free[:] = [ks_f[0], hst]
        sw1_done += ks_i + ks_f + [hst]

    t_sri = bcast("scalar", sreb[0], srec_d[0:1, :], after=sw1_done)
    t_sri = vop(lambda h: h.tensor_scalar_mul(sreb[0], sreb[0], swv[:, 0:1]),
                after=[t_sri, t_swv])
    t_srf = bcast("scalar", sreb[1], srec_d[1:2, :], after=sw1_done)
    t_srf = vop(lambda h: h.tensor_scalar_mul(sreb[1], sreb[1], swv[:, 1:2]),
                after=[t_srf, t_swv])

    lz = vop(lambda h: h.memset(lnf, 0.0))
    scr_store_toks = []
    prev = {"q": [wst_last_read[0], wst_last_read[1]], "scanio_f": None,
            "scanio_i": None, "ev0": None, "ev1": None, "ev2": None, "scano": None}
    for och in range(16):
        qi = stream_quant_w("i", och, 0, after=[prev["q"][0]])
        qf = stream_quant_w("f", och, 1, after=[prev["q"][1]])
        mm = {}
        for wkey, kqs, mqi, bks in (("i", kq_i, 0, (0, 1)), ("f", kq_f, 1, (2, 3))):
            for n in range(2):
                bk = pb[bks[n]]
                bk.wtok = bit_mm(kqs, mq[mqi], bk, n,
                                 after=[qi if wkey == "i" else qf] + sw1_done)
                mm[(wkey, n)] = bk.wtok
        prev["q"] = [mm[("i", 1)], mm[("f", 1)]]
        ef = []
        for n in range(2):
            bk = pb[2 + n]
            e1 = vop(lambda h, bk=bk, n=n: h.tensor_tensor(
                ev[0][:, n * 512:(n + 1) * 512], bk.ap,
                sreb[1][:, n * 512:(n + 1) * 512], OP.mult),
                after=[bk.wtok, t_srf, prev["ev0"]])
            bk.rtok = e1
            ef.append(e1)
        e3 = sop(lambda h: h.activation(scanio[:, 0:TOK], ev[0], AF.Sigmoid),
                 after=ef + [prev["scanio_f"]])
        prev["ev0"] = e3
        ei = []
        for n in range(2):
            bk = pb[0 + n]
            e1 = vop(lambda h, bk=bk, n=n: h.tensor_tensor(
                ev[1][:, n * 512:(n + 1) * 512], bk.ap,
                sreb[0][:, n * 512:(n + 1) * 512], OP.mult),
                after=[bk.wtok, t_sri, prev["ev1"]])
            bk.rtok = e1
            ei.append(e1)
        e5 = sop(lambda h: h.activation(ev[1], ev[1], AF.Silu), after=ei)
        e6 = vop(lambda h: h.tensor_scalar(ev[2], scanio[:, 0:TOK], -1.0, -1.0,
                                           OP.mult, OP.subtract),
                 after=[e3, prev["ev2"]])
        e7 = vop(lambda h: h.tensor_tensor(scanio[:, TOK:2 * TOK], ev[1], ev[2],
                                           OP.mult),
                 after=[e5, e6, prev["scanio_i"]])
        prev["ev1"] = e7
        prev["ev2"] = e7
        s1 = vop(lambda h: h.tensor_tensor_scan(
            scano, scanio[:, 0:TOK], scanio[:, TOK:2 * TOK], 0.0, OP.mult, OP.add),
            after=[e7, prev["scano"]])
        s2 = vop(lambda h, och=och: h.tensor_copy(hcar[:, och:och + 1],
                                                  scano[:, TOK - 1:TOK]), after=[s1])
        prev["scano"] = s2
        f1 = sop(lambda h: h.activation(ev[2], scanio[:, 0:TOK], AF.Ln),
                 after=[e7, e6])
        f2 = vop(lambda h: h.reduce_sum(stat[:, 0:1], ev[2], axis=AX), after=[f1])
        f3 = vop(lambda h, och=och: h.tensor_add(lnf[:, och:och + 1],
                                                 lnf[:, och:och + 1], stat[:, 0:1]),
                 after=[f2, lz])
        prev["ev2"] = f1
        d1 = dma("sync", scr_f[och * 128:(och + 1) * 128, :], scanio[:, 0:TOK],
                 after=[e3])
        d2 = dma("sync", scr_i[och * 128:(och + 1) * 128, :], scanio[:, TOK:2 * TOK],
                 after=[e7])
        prev["scanio_f"] = d1
        prev["scanio_i"] = d2
        prev["ev0"] = e3
        scr_store_toks += [d1, d2]
        sw1_last = [d1, d2, s2, f3]

    t_fexp = sop(lambda h: h.activation(facc, lnf, AF.Exp), after=[sw1_last[3]])
    t_hg1 = dma("sync", cc_hg_in[0, 0:D].rearrange("(j p) -> p j", p=128), facc,
                after=[t_fexp], slow=True)
    t_hg2 = dma("sync", cc_hg_in[0, D:2 * D].rearrange("(j p) -> p j", p=128), hcar,
                after=[sw1_last[2]], slow=True)
    t_ag_hg = ag(cc_hg_in, cc_hg_out, [t_hg1, t_hg2])
    t_cz = vop(lambda h: h.memset(carry, 0.0))
    comb = t_cz
    for j in range(7):
        gF = dma("scalar", ghF, cc_hg_out[j, 0:D].rearrange("(q p) -> p q", p=128),
                 after=[t_ag_hg, comb], slow=True)
        gH = dma("scalar", ghH, cc_hg_out[j, D:2 * D].rearrange("(q p) -> p q", p=128),
                 after=[t_ag_hg, comb], slow=True)
        c1_ = vop(lambda h: h.tensor_mul(gtmp, ghF, carry), after=[gF, comb])
        c2_ = vop(lambda h: h.tensor_add(gtmp, gtmp, ghH), after=[gH, c1_])
        c3_ = vop(lambda h: h.tensor_sub(gtmp, gtmp, carry), after=[c2_])
        c4_ = vop(lambda h, j=j: h.tensor_scalar_mul(gtmp, gtmp, rmk[:, j:j + 1]),
                  after=[c3_, t_rmk])
        comb = vop(lambda h: h.tensor_add(carry, carry, gtmp), after=[c4_])
    t_carry = comb

    if nphase < 2:
        return _finalize(nc, b, taps, y_out)

    t_bng = bcast("scalar", bc[2], nrm["g"][0:1, :], after=sw1_done)
    sw2_done = []
    for t in range(NT):
        tl = dma("sync", tm1, scr_h[t * 128:(t + 1) * 128, :],
                 after=tm1_free + sw1_done)
        ks = quant_tile(tm1, t, bc[2], kq_g, 12, 2, after=[tl, t_bng])
        tm1_free[:] = [ks[0]]
        sw2_done += ks
    t_srg = bcast("scalar", sreb_g, srec_d[2:3, :], after=sw2_done)
    t_srg = vop(lambda h: h.tensor_scalar_mul(sreb_g, sreb_g, swv[:, 2:3]),
                after=[t_srg, t_swv])

    prevq = prev["q"][0]
    g_ev0_free = None
    g_est = None
    g_est_all = []
    for och in range(16):
        qg = stream_quant_w("g", och, och % 2, after=[prevq])
        mmg = []
        for n in range(2):
            bk = pb[n]
            bk.wtok = bit_mm(kq_g, mq[och % 2], bk, n, after=[qg] + sw2_done)
            mmg.append(bk.wtok)
        prevq = mmg[1]
        eg = []
        for n in range(2):
            bk = pb[n]
            e1 = vop(lambda h, bk=bk, n=n: h.tensor_tensor(
                ev2[0][:, n * 512:(n + 1) * 512], bk.ap,
                sreb_g[:, n * 512:(n + 1) * 512], OP.mult),
                after=[bk.wtok, t_srg, g_ev0_free])
            bk.rtok = e1
            eg.append(e1)
        evtoks = []
        for half in range(2):
            bk = pb[2 + half]
            def mkT(half=half, bk=bk):
                def f(h):
                    last = None
                    for j in range(4):
                        tt = half * 4 + j
                        last = h.transpose(bk.ap[:, j * 128:(j + 1) * 128],
                                           ev2[0][:, tt * 128:(tt + 1) * 128], id32)
                    return last
                return f
            bk.wtok = top(mkT(), after=eg + [bk.rtok, t_id32])
            def mkE(half=half, bk=bk):
                def f(h):
                    last = None
                    for j in range(4):
                        last = h.tensor_copy(ev2[1][:, (half * 4 + j) * 128:
                                                    (half * 4 + j + 1) * 128],
                                             bk.ap[:, j * 128:(j + 1) * 128])
                    return last
                return f
            bk.rtok = vop(mkE(), after=[bk.wtok, g_est])
            evtoks.append(bk.rtok)
        g_ev0_free = pb[3].wtok
        g_est = b.d("sync",
                    lambda h, och=och: [h.dma_start(
                        out=scr_g[tt * 128:(tt + 1) * 128, och * 128:(och + 1) * 128],
                        in_=ev2[1][:, tt * 128:(tt + 1) * 128]) for tt in range(8)],
                    8, after=evtoks)
        g_est_all.append(g_est)

    if nphase < 3:
        return _finalize(nc, b, taps, y_out)

    t_bgn = bcast("scalar", bc[3], gnw_in[0:1, :], after=sw2_done)
    t_bno = bcast("scalar", bc[1], nrm["o"][0:1, :], after=sw1_done)
    carryB = carry
    pb_done = []
    prev_sc = [t_carry]
    htile_free = [None]
    gtile_free = [None]
    for t in range(NT):
        httoks = []
        for och in range(16):
            lf = dma("sync", scn2[:, 0:128],
                     scr_f[och * 128:(och + 1) * 128, t * 128:(t + 1) * 128],
                     after=list(prev_sc) + scr_store_toks)
            li = dma("sync", scn2[:, 128:256],
                     scr_i[och * 128:(och + 1) * 128, t * 128:(t + 1) * 128],
                     after=list(prev_sc) + scr_store_toks)
            s1 = vop(lambda h, och=och: h.tensor_tensor_scan(
                scn2o, scn2[:, 0:128], scn2[:, 128:256],
                carryB[:, och:och + 1], OP.mult, OP.add),
                after=[lf, li, t_carry])
            s2 = vop(lambda h, och=och: h.tensor_copy(
                carryB[:, och:och + 1], scn2o[:, 127:128]), after=[s1])
            bk = pb[4 + (och % 2)]
            bk.wtok = top(lambda h, bk=bk: h.transpose(
                bk.ap[:, 0:128], scn2o, id32), after=[s1, bk.rtok, t_id32])
            bk.rtok = vop(lambda h, bk=bk, och=och: h.tensor_copy(
                htile[:, och * 128:(och + 1) * 128], bk.ap[:, 0:128]),
                after=[bk.wtok, htile_free[0] if och == 0 else None])
            httoks.append(bk.rtok)
            prev_sc = [s2, bk.wtok]
        lg = dma("sync", gtile, scr_g[t * 128:(t + 1) * 128, :],
                 after=g_est_all + [gtile_free[0]])
        r2 = sop(lambda h: h.activation(tm0, gtile, AF.Square),
                 after=[lg, tm0_free[0]])
        cbr = 32 * (t % 2) + 26
        rst = lambda c: statw[:, cbr + c:cbr + c + 1]
        rguard = list(statg[t % 2])
        r3 = vop(lambda h: h.reduce_sum(rst(0), tm0, axis=AX),
                 after=[r2] + rguard)
        r8 = sop(lambda h: h.activation(rst(1), rst(0), AF.Sqrt,
                                        bias=cvec[:, 0:1], scale=float(1.0 / D)),
                 after=[r3, t_cvec])
        r9 = vop(lambda h: h.reciprocal(rst(2), rst(1)), after=[r8])
        ra = vop(lambda h: h.scalar_tensor_tensor(gtile, gtile, rst(2), bc[3],
                                                  OP.mult, OP.mult),
                 after=[r9, t_bgn, r2])
        statg[t % 2].append(r8)
        tdb1 = None
        if "dbg1" in taps:
            tdb1 = dma("sync", dbg1[t * 128:(t + 1) * 128, :], htile, after=httoks)
        rb = sop(lambda h: h.activation(tm1, htile, AF.Silu),
                 after=httoks + tm1_free)
        htile_free[0] = tdb1 or rb
        rc = vop(lambda h: h.tensor_tensor(tm1, tm1, gtile, OP.mult), after=[rb, ra])
        gtile_free[0] = rc
        tdb2 = None
        if "dbg2" in taps:
            tdb2 = dma("sync", dbg2[t * 128:(t + 1) * 128, :], tm1, after=[rc])
        ks = quant_tile(tm1, t, bc[1], kq_o, 12, None, after=[rc, t_bno],
                        keep_invs=3)
        tm1_free[:] = [ks[0]] + ([tdb2] if tdb2 else [])
        pb_done += ks
    sw3_done = pb_done

    if "dbg3" in taps:
        dma("sync", dbg3, kq_o, after=sw3_done)

    t_bga = bcast("scalar", bc[0], modL[0][0:1, 2 * D:3 * D],
                  after=[t_mod[0]] + sw1_done)
    t_m = vop(lambda h: h.tensor_scalar(bc[2], bc[0], 0.0, None, OP.is_ge),
              after=[t_bga] + sw2_done)
    t_a = vop(lambda h: h.tensor_scalar_max(bc[3], bc[0], 0.25),
              after=[t_bga, gtile_free[0], t_m])
    t_b = vop(lambda h: h.tensor_scalar_min(bc[0], bc[0], -0.25), after=[t_a])
    t_c1 = vop(lambda h: h.tensor_sub(bc[3], bc[3], bc[0]), after=[t_b])
    t_c2 = vop(lambda h: h.tensor_mul(bc[3], bc[3], bc[2]), after=[t_c1])
    t_ga = vop(lambda h: h.tensor_add(bc[0], bc[0], bc[3]), after=[t_c2])
    if "dbg4" in taps:
        tdga = dma("sync", dbg4[:, 0:D], bc[0], after=[t_ga])
        dma("sync", dbg4[:, 0:NT], oinv, after=sw3_done + [tdga])
    if "dbg5" in taps:
        dma("sync", dbg5[0:1, 0:8], sc1[0:1, 0:8], after=[t_swst])

    import os as _os
    if _os.environ.get("WO_SERIAL") == "1":
        b.serial = True
    prev_x1 = [None] * 2
    x1_store_toks = []
    prev_dump = None
    for och in range(16):
        qo = stream_quant_w("o", och, och % 2, after=[prevq])
        mmo = []
        for n in range(2):
            bk = pb[n]
            bk.wtok = bit_mm(kq_o, mq[och % 2], bk, n, after=[qo] + sw3_done)
            mmo.append(bk.wtok)
        prevq = mmo[1]
        eo = []
        for n in range(2):
            bk = pb[n]
            e1 = vop(lambda h, bk=bk, n=n: h.tensor_copy(
                ev2[0][:, n * 512:(n + 1) * 512], bk.ap), after=[bk.wtok, prev_dump])
            bk.rtok = e1
            eo.append(e1)
        if "dbg6" in taps:
            prev_dump = dma("sync", dbg6[och * 128:(och + 1) * 128, :], ev2[0],
                            after=eo)
        for half in range(2):
            bk = pb[2 + half]
            def mkT(half=half, bk=bk):
                def f(h):
                    last = None
                    for j in range(4):
                        tt = half * 4 + j
                        last = h.transpose(bk.ap[:, j * 128:(j + 1) * 128],
                                           ev2[0][:, tt * 128:(tt + 1) * 128], id32)
                    return last
                return f
            bk.wtok = top(mkT(), after=eo + [bk.rtok, t_id32])
            exs = []
            for j in range(4):
                tt = half * 4 + j
                lx = dma("scalar", tm0[:, (half * 4 + j) * 128:
                                       (half * 4 + j + 1) * 128],
                         x_in[tt * 128:(tt + 1) * 128, och * 128:(och + 1) * 128],
                         after=[tm0_free[0], prev_x1[half]])
                exs.append(lx)

            def mkE(half=half, bk=bk, och=och):
                def f(h):
                    last = None
                    for j in range(4):
                        tt = half * 4 + j
                        sl = slice(tt * 128, (tt + 1) * 128)
                        last = h.scalar_tensor_tensor(
                            ev2[1][:, sl], bk.ap[:, j * 128:(j + 1) * 128],
                            oinv[:, tt:tt + 1],
                            bc[0][:, och * 128:(och + 1) * 128],
                            OP.mult, OP.mult)
                    return last
                return f
            tev = vop(mkE(), after=[bk.wtok, t_ga, prev_x1[half]] + g_est_all)
            def mkE2(half=half):
                def f(h):
                    last = None
                    for j in range(4):
                        tt = half * 4 + j
                        sl = slice(tt * 128, (tt + 1) * 128)
                        last = h.tensor_add(ev2[1][:, sl], ev2[1][:, sl],
                                            tm0[:, sl])
                    return last
                return f
            tev2 = vop(mkE2(), after=[tev] + exs)
            bk.rtok = tev2
            tm0_free[0] = tev2
            if "dbg7" in taps:
                tev2 = dma("sync", dbg7[och * 128:(och + 1) * 128,
                                        half * 512:(half + 1) * 512],
                           ev2[1][:, half * 512:(half + 1) * 512], after=[tev2])
            st = b.d("sync",
                     lambda h, half=half, och=och: [h.dma_start(
                         out=xres[(half * 4 + j) * 128:(half * 4 + j + 1) * 128,
                                  och * 128:(och + 1) * 128],
                         in_=ev2[1][:, (half * 4 + j) * 128:(half * 4 + j + 1) * 128])
                         for j in range(4)],
                     4, after=[tev2])
            prev_x1[half] = st
            x1_store_toks.append(st)
    t_x1 = x1_store_toks

    if nphase < 4:
        return _finalize(nc, b, taps, y_out)

    b.serial = False
    l0_done = x1_store_toks + sw3_done + [t_ga]

    q8t = at("q8t_s", OFF_P + 0, (128, D), FP8)
    q8b = at("q8b_s", OFF_P + 2 * KB, (128, D), BF16)
    insb = small("insbs", (1, 1), BF16)
    t_lsi = dma("scalar", insb, ins_in[0:1, 0:1])
    t_lw8 = dma("scalar", sc1[0:1, 18:19], w8s_in[0:1, 0:1])
    t_s32 = vop(lambda h: h.tensor_copy(sc1[0:1, 16:17], insb), after=[t_lsi])
    t_rs = vop(lambda h: h.reciprocal(sc1[0:1, 17:18], sc1[0:1, 16:17]),
               after=[t_s32])
    t_c = vop(lambda h: h.tensor_mul(sc1[0:1, 18:19], sc1[0:1, 18:19],
                                     sc1[0:1, 16:17]), after=[t_s32, t_lw8])
    t_scst = dma("sync", smallv[0:1, 16:19], sc1[0:1, 16:19], after=[t_rs, t_c])
    t_scbc = dma("scalar", c128[:, 0:3],
                 smallv[0:1, 16:19].broadcast_to((128, 3)), after=[t_scst])
    t_fb1 = bcast("scalar", bc[1], modL[0][0:1, 4 * D:5 * D],
                  after=[t_mod[0]] + l0_done)
    t_fb1 = vop(lambda h: h.tensor_scalar_add(bc[1], bc[1], 1.0), after=[t_fb1])
    t_fb2 = bcast("scalar", bc[2], modL[0][0:1, 3 * D:4 * D],
                  after=[t_mod[0]] + l0_done)
    t_fb3 = bcast("scalar", bc[3], modL[0][0:1, 5 * D:6 * D],
                  after=[t_mod[0]] + l0_done)
    t_gm = vop(lambda h: h.tensor_scalar(bc[0], bc[3], 0.0, None, OP.is_ge),
               after=[t_fb3] + l0_done)
    t_ga2_ = vop(lambda h: h.tensor_scalar_max(wst[0], bc[3], 0.25),
                 after=[t_fb3, t_gm])
    t_gb2_ = vop(lambda h: h.tensor_scalar_min(bc[3], bc[3], -0.25), after=[t_ga2_])
    t_gc1 = vop(lambda h: h.tensor_sub(wst[0], wst[0], bc[3]), after=[t_gb2_])
    t_gc2 = vop(lambda h: h.tensor_mul(wst[0], wst[0], bc[0]), after=[t_gc1])
    t_gb = vop(lambda h: h.tensor_add(bc[3], bc[3], wst[0]), after=[t_gc2])
    t_gbe = vop(lambda h: h.tensor_scalar_mul(bc[3], bc[3], c128[:, 2:3]),
                after=[t_gb, t_scbc])

    ffn_q_done = []
    for t in range(NT):
        tl = dma("sync", tm0, xres[t * 128:(t + 1) * 128, :],
                 after=[tm0_free[0]] + x1_store_toks)
        tm0_free[0] = tl
        th = None
        cb2 = 32 * (t % 2)
        st_ = lambda c, cb2=cb2: statw[:, cb2 + c:cb2 + c + 1]
        guard = list(statg[t % 2])
        statg[t % 2] = []
        t1 = vop(lambda h, st_=st_: h.reduce_sum(st_(1), tm0, axis=AX),
                 after=[tl] + guard)
        t2 = sop(lambda h: h.activation(tm1, tm0, AF.Square), after=[tl] + tm1_free)
        t3 = vop(lambda h, st_=st_: h.reduce_sum(st_(2), tm1, axis=AX), after=[t2])
        t4 = vop(lambda h, st_=st_: h.tensor_scalar_mul(st_(1), st_(1),
                                                        float(1.0 / D)), after=[t1])
        t5 = vop(lambda h, st_=st_: h.tensor_mul(st_(3), st_(1), st_(1)), after=[t4])
        t6 = vop(lambda h, st_=st_: h.tensor_scalar_mul(st_(2), st_(2),
                                                        float(1.0 / D)), after=[t3])
        t7 = vop(lambda h, st_=st_: h.tensor_sub(st_(2), st_(2), st_(3)),
                 after=[t5, t6])
        t8 = sop(lambda h, st_=st_: h.activation(st_(4), st_(2), AF.Sqrt,
                                                 bias=cvec[:, 0:1]),
                 after=[t7, t_cvec])
        t9 = vop(lambda h, st_=st_: h.reciprocal(st_(5), st_(4)), after=[t8])
        ta = vop(lambda h, st_=st_: h.tensor_scalar(tm1, tm0, st_(1), st_(5),
                                                    OP.subtract, OP.mult),
                 after=[t9, t3])
        tb = vop(lambda h: h.tensor_tensor(tm1, tm1, bc[1], OP.mult),
                 after=[ta, t_fb1])
        tc = vop(lambda h: h.tensor_tensor(tm1, tm1, bc[2], OP.add),
                 after=[tb, t_fb2])
        statg[t % 2].append(t8)
        tm0_free[0] = ta
        q1 = vop(lambda h: h.tensor_scalar(tm1, tm1, c128[:, 1:2], 448.0,
                                           OP.mult, OP.min), after=[tc, t_scbc])
        q2 = vop(lambda h: h.tensor_scalar_max(tm1, tm1, -448.0), after=[q1])
        q3 = sop(lambda h: h.copy(q8t, tm1), after=[q2] + ffn_q_done[-1:])
        q4 = sop(lambda h: h.copy(q8b, q8t), after=[q3])
        tm1_free[:] = [q3]
        for half in range(2):
            bkp = pt[half]
            def mkT(half=half, bkp=bkp):
                def f(h):
                    last = None
                    for j in range(8):
                        dc = half * 8 + j
                        last = h.transpose(bkp.ap[:, j * 128:(j + 1) * 128],
                                           q8b[:, dc * 128:(dc + 1) * 128], idbf)
                    return last
                return f
            bkp.wtok = top(mkT(), after=[q4, bkp.rtok, t_idbf])
            def mkE(half=half, bkp=bkp, t=t):
                def f(h):
                    last = None
                    for j in range(8):
                        dc = half * 8 + j
                        last = h.tensor_copy(
                            q8T[:, dc * TOK + t * 128:dc * TOK + (t + 1) * 128],
                            bkp.ap[:, j * 128:(j + 1) * 128])
                    return last
                return f
            bkp.rtok = vop(mkE(), after=[bkp.wtok])
            ffn_q_done.append(bkp.rtok)

    x2_store_toks = []
    prev_w8 = [prevq, prevq]
    prev_x2 = {}
    prev_ev20 = [x1_store_toks[-1]]
    for n in range(4):
        for tg in range(2):
            mm_group = []
            for dc in range(DC):
                bi = dc % 2
                tw = b.d("gpsimd",
                         lambda h, dc=dc, n=n, bi=bi: [h.dma_start(
                             out=w8b[bi],
                             in_=W8T[dc * 128:(dc + 1) * 128,
                                     n * 512:(n + 1) * 512])],
                         1, after=[prev_w8[bi]])
                def mkM(dc=dc, n=n, tg=tg, bi=bi):
                    def f(h):
                        last = None
                        for j in range(4):
                            t = tg * 4 + j
                            last = h.matmul(
                                pb[j].ap,
                                q8T[:, dc * TOK + t * 128:dc * TOK + (t + 1) * 128],
                                w8b[bi],
                                start=(dc == 0), stop=(dc == DC - 1))
                        return last
                    return f
                mmt = top(mkM(), after=[tw] + ffn_q_done +
                          [pb[j].rtok for j in range(4)])
                prev_w8[bi] = mmt
                mm_group.append(mmt)
            for j in range(4):
                pb[j].wtok = mm_group[-1]
            for j in range(4):
                t = tg * 4 + j
                bkj = pb[j]
                lx = dma("scalar", tm0[:, j * 512:(j + 1) * 512],
                         xres[t * 128:(t + 1) * 128, n * 512:(n + 1) * 512],
                         after=[tm0_free[0], prev_x2.get((t, n), None)] +
                               x1_store_toks)
                e1 = vop(lambda h, bkj=bkj, n=n: h.tensor_tensor(
                    ev2[0][:, 0:512],
                    bkj.ap, bc[3][:, n * 512:(n + 1) * 512],
                    OP.mult), after=[bkj.wtok, t_gbe, prev_ev20[0]])
                e2 = vop(lambda h, j=j: h.tensor_add(
                    ev2[0][:, 0:512], ev2[0][:, 0:512],
                    tm0[:, j * 512:(j + 1) * 512]),
                    after=[e1, lx])
                bkj.rtok = e2
                tm0_free[0] = e2
                st2 = dma("sync", xres[t * 128:(t + 1) * 128, n * 512:(n + 1) * 512],
                          ev2[0][:, 0:512], after=[e2])
                x2_store_toks.append(st2)
                prev_x2[(t, n)] = st2
                prev_ev20[0] = st2

    if nphase < 5:
        return _finalize(nc, b, taps, y_out)

    attn_gate = x2_store_toks + [t_gbe]
    kvstrip = at("kvstrip", OFF_P + 0, (128, TOK), BF16)
    sqstrip = at("sqstrip", OFF_P + 2 * KB, (128, TOK), BF16)
    vtstrip = at("vtstrip", OFF_P + 4 * KB, (128, TOK), BF16)
    onb = small("onbs", (128, 1), BF16)
    lacc = small("laccs", (128, NT))
    t_onb = vop(lambda h: h.memset(onb, 1.0))

    t_ab1 = bcast("scalar", bc[1], modL[1][0:1, D:2 * D], after=[t_mod[1]] + attn_gate)
    t_ab1 = vop(lambda h: h.tensor_scalar_add(bc[1], bc[1], 1.0), after=[t_ab1])
    t_ab2 = bcast("scalar", bc[2], modL[1][0:1, 0:D], after=[t_mod[1]] + attn_gate)

    hfm_done = []
    for t in range(NT):
        tl = dma("sync", tm0, xres[t * 128:(t + 1) * 128, :],
                 after=[tm0_free[0]] + x2_store_toks)
        tm0_free[0] = tl
        cb2 = 32 * (t % 2)
        st_ = lambda c, cb2=cb2: statw[:, cb2 + c:cb2 + c + 1]
        guard = list(statg[t % 2])
        statg[t % 2] = []
        t1 = vop(lambda h, st_=st_: h.reduce_sum(st_(1), tm0, axis=AX),
                 after=[tl] + guard)
        t2 = sop(lambda h: h.activation(tm1, tm0, AF.Square), after=[tl] + tm1_free)
        t3 = vop(lambda h, st_=st_: h.reduce_sum(st_(2), tm1, axis=AX), after=[t2])
        t4 = vop(lambda h, st_=st_: h.tensor_scalar_mul(st_(1), st_(1),
                                                        float(1.0 / D)), after=[t1])
        t5 = vop(lambda h, st_=st_: h.tensor_mul(st_(3), st_(1), st_(1)), after=[t4])
        t6 = vop(lambda h, st_=st_: h.tensor_scalar_mul(st_(2), st_(2),
                                                        float(1.0 / D)), after=[t3])
        t7 = vop(lambda h, st_=st_: h.tensor_sub(st_(2), st_(2), st_(3)),
                 after=[t5, t6])
        t8 = sop(lambda h, st_=st_: h.activation(st_(4), st_(2), AF.Sqrt,
                                                 bias=cvec[:, 0:1]),
                 after=[t7, t_cvec])
        t9 = vop(lambda h, st_=st_: h.reciprocal(st_(5), st_(4)), after=[t8])
        ta = vop(lambda h, st_=st_: h.tensor_scalar(tm1, tm0, st_(1), st_(5),
                                                    OP.subtract, OP.mult),
                 after=[t9, t3])
        tb = vop(lambda h: h.tensor_tensor(tm1, tm1, bc[1], OP.mult),
                 after=[ta, t_ab1])
        tc = vop(lambda h: h.tensor_tensor(tm1, tm1, bc[2], OP.add),
                 after=[tb, t_ab2])
        statg[t % 2].append(t8)
        tm0_free[0] = ta
        for half in range(2):
            bk = pb[4 + half]
            for qtr in range(2):
                def mkT2(half=half, bk=bk, qtr=qtr):
                    def f(h):
                        last = None
                        for j in range(4):
                            dc = half * 8 + qtr * 4 + j
                            last = h.transpose(bk.ap[:, j * 128:(j + 1) * 128],
                                               tm1[:, dc * 128:(dc + 1) * 128], id32)
                        return last
                    return f
                bk.wtok = top(mkT2(), after=[tc, bk.rtok, t_id32])
                def mkE(half=half, bk=bk, qtr=qtr, t=t):
                    def f(h):
                        last = None
                        for j in range(4):
                            dc = half * 8 + qtr * 4 + j
                            last = h.tensor_copy(
                                hfm[:, dc * TOK + t * 128:dc * TOK + (t + 1) * 128],
                                bk.ap[:, j * 128:(j + 1) * 128])
                        return last
                    return f
                bk.rtok = vop(mkE(), after=[bk.wtok] + (attn_gate if t == 0 else []))
                hfm_done.append(bk.rtok)
        tm1_free[:] = [hfm_done[-1]]

    prev_wq = [x2_store_toks[-1], x2_store_toks[-1]]
    k_stage = []
    v_stage = []
    q_done = []
    kn_mm = None
    qn_mm = None
    strip_free = [x2_store_toks[-1]]
    vt_free = [x2_store_toks[-1]]
    ksq_sb = small("ksq_sbs", (1, TOK))
    qn_sb2 = small("qn2s", (1, TOK))
    ksqb = small("ksqbs", (1, 32), BF16)
    mk_bf = small("mk_bfs", (1, 8), BF16)

    def qkv_block(blk, after_mm=()):
        bi = blk % 2
        tw = b.d("gpsimd",
                 lambda h, blk=blk, bi=bi: [h.dma_start(
                     out=mq[bi].rearrange("p (c q) -> p c q", q=128),
                     in_=WqkvT[:, blk * 128:(blk + 1) * 128].rearrange(
                         "(c p) q -> p c q", p=128))],
                 1, after=[prev_wq[bi]])
        mmq = []
        for n in range(2):
            bk = pb[n]
            def mkM(bi=bi, n=n):
                def f(h):
                    last = None
                    for dc in range(DC):
                        last = h.matmul(
                            pb[n].ap, mq[bi][:, dc * 128:(dc + 1) * 128],
                            hfm[:, dc * TOK + n * 512:dc * TOK + (n + 1) * 512],
                            start=(dc == 0), stop=(dc == DC - 1))
                    return last
                return f
            bk.wtok = top(mkM(), after=[tw, bk.rtok] + hfm_done + list(after_mm))
            mmq.append(bk.wtok)
        prev_wq[bi] = mmq[1]
        return mmq

    for kb in range(16):
        blk = 16 + kb
        qkv_block(blk)
        ev_t = []
        for n in range(2):
            bk = pb[n]
            e1 = vop(lambda h, bk=bk, n=n, blk=blk: h.tensor_scalar_add(
                kvstrip[:, n * 512:(n + 1) * 512],
                bk.ap, bq_sb[:, blk:blk + 1]),
                after=[bk.wtok, t_bq, strip_free[0]] +
                      (x2_store_toks if kb == 0 else []))
            bk.rtok = e1
            ev_t.append(e1)
        sq = sop(lambda h: h.activation(sqstrip, kvstrip, AF.Square),
                 after=ev_t + ([kn_mm] if kn_mm else []))
        def mkN(kb=kb):
            def f(h):
                h.matmul(pb[4].ap[0:1, :], onb, sqstrip[:, 0:512],
                         start=(kb == 0), stop=(kb == 15))
                return h.matmul(pb[5].ap[0:1, :], onb, sqstrip[:, 512:1024],
                                start=(kb == 0), stop=(kb == 15))
            return f
        kn_mm = top(mkN(), after=[sq, t_onb, pb[4].rtok, pb[5].rtok])
        kst = dma("sync",
                  cc_k_in[0, kb * 128 * TOK:(kb + 1) * 128 * TOK].rearrange(
                      "(p t) -> p t", t=TOK),
                  kvstrip, after=ev_t)
        k_stage.append(kst)
        strip_free[0] = kst
    e_k1 = vop(lambda h: h.tensor_copy(ksq_sb[0:1, 0:512], pb[4].ap[0:1, :]),
               after=[kn_mm])
    e_k2 = vop(lambda h: h.tensor_copy(ksq_sb[0:1, 512:1024], pb[5].ap[0:1, :]),
               after=[kn_mm])
    pb[4].rtok = e_k1
    pb[5].rtok = e_k2
    e_k3 = vop(lambda h: h.reduce_max(sc1[0:1, 21:22], ksq_sb, axis=AX),
               after=[e_k1, e_k2])
    e_k0 = vop(lambda h: h.memset(ksqb, 0.0))
    e_k4 = vop(lambda h: h.tensor_copy(ksqb[0:1, 0:1], sc1[0:1, 21:22]),
               after=[e_k3, e_k0])
    kst2 = dma("sync", cc_k_in[0, D * TOK:D * TOK + 32].rearrange(
        "(o n) -> o n", o=1), ksqb, after=[e_k4])
    t_ag_k = ag(cc_k_in, cc_k_out, k_stage + [kst2])

    for vb in range(16):
        blk = 32 + vb
        qkv_block(blk)
        ev_t = []
        for n in range(2):
            bk = pb[n]
            e1 = vop(lambda h, bk=bk, n=n, blk=blk: h.tensor_scalar_add(
                kvstrip[:, n * 512:(n + 1) * 512],
                bk.ap, bq_sb[:, blk:blk + 1]),
                after=[bk.wtok, t_bq, strip_free[0]])
            bk.rtok = e1
            ev_t.append(e1)
        for half in range(2):
            bkp = pt[half]
            def mkT(half=half, bkp=bkp):
                def f(h):
                    last = None
                    for j in range(4):
                        tt = half * 4 + j
                        last = h.transpose(bkp.ap[:, j * 128:(j + 1) * 128],
                                           kvstrip[:, tt * 128:(tt + 1) * 128],
                                           idbf)
                    return last
                return f
            bkp.wtok = top(mkT(), after=ev_t + [bkp.rtok, t_idbf])
            def mkE(half=half, bkp=bkp):
                def f(h):
                    last = None
                    for j in range(4):
                        tt = half * 4 + j
                        last = h.tensor_copy(
                            vtstrip[:, tt * 128:(tt + 1) * 128],
                            bkp.ap[:, j * 128:(j + 1) * 128])
                    return last
                return f
            bkp.rtok = vop(mkE(), after=[bkp.wtok, vt_free[0]])
        vst = b.d("sync",
                  lambda h, vb=vb: [h.dma_start(
                      out=ccv_i[0, :].rearrange("(t d) -> t d", d=D)[
                          tt * 128:(tt + 1) * 128, vb * 128:(vb + 1) * 128],
                      in_=vtstrip[:, tt * 128:(tt + 1) * 128])
                      for tt in range(8)],
                  8, after=[pt[0].rtok, pt[1].rtok])
        v_stage.append(vst)
        vt_free[0] = vst
        strip_free[0] = pt[1].wtok
    t_ag_v = ag(ccv_i, ccv_o, v_stage + [t_ag_k])

    for blk in range(16):
        qkv_block(blk)
        ev_t = []
        for n in range(2):
            bk = pb[n]
            e1 = vop(lambda h, bk=bk, n=n, blk=blk: h.tensor_scalar_add(
                Qfm[:, blk * TOK + n * 512:blk * TOK + (n + 1) * 512],
                bk.ap, bq_sb[:, blk:blk + 1]),
                after=[bk.wtok, t_bq] + (x2_store_toks if blk == 0 else []))
            bk.rtok = e1
            ev_t.append(e1)
        sq = sop(lambda h, blk=blk: h.activation(
            sqstrip, Qfm[:, blk * TOK:(blk + 1) * TOK], AF.Square),
            after=ev_t + [strip_free[0]])
        def mkN(blk=blk):
            def f(h):
                h.matmul(pb[4].ap[0:1, :], onb, sqstrip[:, 0:512],
                         start=(blk == 0), stop=(blk == 15))
                return h.matmul(pb[5].ap[0:1, :], onb, sqstrip[:, 512:1024],
                                start=(blk == 0), stop=(blk == 15))
            return f
        qn_mm = top(mkN(), after=[sq, t_onb, pb[4].rtok, pb[5].rtok])
        strip_free[0] = qn_mm
        q_done += ev_t
    e_q1 = vop(lambda h: h.tensor_copy(qn_sb2[0:1, 0:512], pb[4].ap[0:1, :]),
               after=[qn_mm])
    e_q2 = vop(lambda h: h.tensor_copy(qn_sb2[0:1, 512:1024], pb[5].ap[0:1, :]),
               after=[qn_mm])
    pb[4].rtok = e_q1
    pb[5].rtok = e_q2
    e_q3 = sop(lambda h: h.activation(qn_sb2, qn_sb2, AF.Sqrt), after=[e_q1, e_q2])
    qnst = dma("sync", qn_d[0:1, :], qn_sb2, after=[e_q3])
    qnld = dma("scalar", qn_sb, qn_d[0, :].rearrange("(j p) -> p j", p=128),
               after=[qnst], slow=True)
    mk_ld2 = dma("scalar", mk_bf,
                 cc_k_out.rearrange("r c -> (r c)")[
                     D * TOK::KVC1].rearrange("(o n) -> o n", o=1),
                 after=[t_ag_k], slow=True)
    e_m1 = vop(lambda h: h.tensor_copy(srow[0:1, 0:8], mk_bf), after=[mk_ld2])
    e_m2 = vop(lambda h: h.reduce_max(sc1[0:1, 22:23], srow[0:1, 0:8], axis=AX),
               after=[e_m1])
    e_m3 = sop(lambda h: h.activation(sc1[0:1, 23:24], sc1[0:1, 22:23], AF.Sqrt),
               after=[e_m2])
    mkst = dma("sync", smallv[0:1, 23:24], sc1[0:1, 23:24], after=[e_m3])
    mkbc = dma("scalar", c128[:, 3:4], smallv[0:1, 23:24].broadcast_to((128, 1)),
               after=[mkst])
    t_bias = vop(lambda h: h.tensor_scalar(biasB, qn_sb, c128[:, 3:4],
                                           float(-ISQD), OP.mult, OP.mult),
                 after=[qnld, mkbc])

    t_lz = vop(lambda h: h.memset(lacc, 0.0))
    ctx_z = []
    last_qkv_mm = prev_wq[0] if prev_wq[0][1] > prev_wq[1][1] else prev_wq[1]
    for t in range(NT):
        ctx_z.append(vop(lambda h, t=t: h.memset(ctx[t], 0.0),
                         after=[last_qkv_mm] + q_done))
    kb_cons = {}
    prev_lp = [None] * NT
    pP_free = [None, None]
    ptq_free = [x2_store_toks[-1], x2_store_toks[-1]]
    kbuf_free = [None, None]
    vbuf_free = [None, None]
    sc_done = []
    for kb in range(32):
        r, off = kb // 4, (kb % 4) * 256
        kl = dma("sync", kbuf[kb % 2].rearrange("p (c t) -> p c t", t=256),
                 cc_k_out[r, 0:D * TOK].rearrange(
                     "(c p t) -> p c t", p=128, t=TOK)[:, :, off:off + 256],
                 after=[t_ag_k, kbuf_free[kb % 2], qn_mm] + q_done)
        vl = dma("sync", vbuf[kb % 2].rearrange("p (tc d) -> p tc d", d=D),
                 ccv_o[r, :].rearrange(
                     "(tc p d) -> p tc d", p=128, d=D)[:, 2 * (kb % 4):2 * (kb % 4) + 2, :],
                 after=[t_ag_v, vbuf_free[kb % 2]])
        for q in range(NT):
            sbk = pb[q % 2]
            def mkS(kb=kb, q=q):
                def f(h):
                    last = None
                    for dc in range(DC):
                        last = h.matmul(
                            pb[q % 2].ap[:, 0:256],
                            Qfm[:, dc * TOK + q * 128:dc * TOK + (q + 1) * 128],
                            kbuf[kb % 2][:, dc * 256:(dc + 1) * 256],
                            start=(dc == 0), stop=(dc == DC - 1))
                    return last
                return f
            sbk.wtok = top(mkS(), after=[kl, sbk.rtok] + q_done)
            ex = sop(lambda h, q=q, kb=kb: h.activation(
                pP[q % 2][:, 0:256], pb[q % 2].ap[:, 0:256], AF.Exp,
                bias=biasB[:, q:q + 1], scale=float(ISQD),
                accum_out=lpart[:, q:q + 1]),
                after=[sbk.wtok, t_bias, prev_lp[q], pP_free[q % 2]])
            sbk.rtok = ex
            la = vop(lambda h, q=q: h.tensor_add(lacc[:, q:q + 1], lacc[:, q:q + 1],
                                                 lpart[:, q:q + 1]),
                     after=[ex, t_lz])
            prev_lp[q] = la
            bkp = pt[q % 2]
            def mkPT(q=q):
                def f(h):
                    h.transpose(pt[q % 2].ap[:, 0:128], pP[q % 2][:, 0:128], idbf)
                    return h.transpose(pt[q % 2].ap[:, 128:256],
                                       pP[q % 2][:, 128:256], idbf)
                return f
            bkp.wtok = top(mkPT(), after=[ex, bkp.rtok, t_idbf])
            pe = vop(lambda h, q=q: h.tensor_copy(ptq[q % 2][:, 0:256],
                                                  pt[q % 2].ap[:, 0:256]),
                     after=[bkp.wtok, ptq_free[q % 2]])
            bkp.rtok = pe
            pP_free[q % 2] = bkp.wtok
            pv_toks = []
            for n in range(4):
                pvb = pb[2 + n % 2]
                def mkPV(q=q, n=n, kb=kb):
                    def f(h):
                        last = None
                        for kc in range(2):
                            last = h.matmul(
                                pb[2 + n % 2].ap,
                                ptq[q % 2][:, kc * 128:(kc + 1) * 128],
                                vbuf[kb % 2].rearrange(
                                    "p (tc d) -> p tc d", d=D)[:, kc,
                                                              n * 512:(n + 1) * 512],
                                start=(kc == 0), stop=(kc == 1))
                        return last
                    return f
                pvb.wtok = top(mkPV(), after=[vl, pe, pvb.rtok])
                ca = vop(lambda h, q=q, n=n, pvb=pvb: h.tensor_tensor(
                    ctx[q][:, n * 512:(n + 1) * 512],
                    ctx[q][:, n * 512:(n + 1) * 512], pvb.ap, OP.add),
                    after=[pvb.wtok, ctx_z[q]])
                pvb.rtok = ca
                pv_toks.append(ca)
            ptq_free[q % 2] = pv_toks[-1]
            kb_cons.setdefault(kb, []).extend([sbk.wtok, pv_toks[-1]])
        kbuf_free[kb % 2] = kb_cons[kb][-2]
        vbuf_free[kb % 2] = kb_cons[kb][-1]
        sc_done = kb_cons[kb]
    t_lrec = vop(lambda h: h.reciprocal(lrec[:, 0:NT], lacc[:, 0:NT]),
                 after=[prev_lp[q] for q in range(NT)])

    ctxf = at("ctxf", OFF_A, (128, DC * TOK), BF16)
    ctxf_done = []
    for q in range(NT):
        for half in range(2):
            for qtr in range(2):
                bk = pb[4 + half]
                def mkT(q=q, half=half, qtr=qtr, bk=bk):
                    def f(h):
                        last = None
                        for j in range(4):
                            dc = half * 8 + qtr * 4 + j
                            last = h.transpose(bk.ap[:, j * 128:(j + 1) * 128],
                                               ctx[q][:, dc * 128:(dc + 1) * 128],
                                               id32)
                        return last
                    return f
                bk.wtok = top(mkT(), after=[bk.rtok, t_id32] + sc_done +
                              ([prev_lp[q]] if qtr == 0 and half == 0 else []))
                def mkE(q=q, half=half, qtr=qtr, bk=bk):
                    def f(h):
                        last = None
                        for j in range(4):
                            dc = half * 8 + qtr * 4 + j
                            last = h.tensor_copy(
                                ctxf[:, dc * TOK + q * 128:dc * TOK + (q + 1) * 128],
                                bk.ap[:, j * 128:(j + 1) * 128])
                        return last
                    return f
                bk.rtok = vop(mkE(), after=[bk.wtok])
                ctxf_done.append(bk.rtok)

    t_ob0 = bcast("scalar", bc[0], modL[1][0:1, 2 * D:3 * D],
                  after=[t_mod[1]] + sc_done)
    t_om = vop(lambda h: h.tensor_scalar(bc[2], bc[0], 0.0, None, OP.is_ge),
               after=[t_ob0] + sc_done)
    t_oa = vop(lambda h: h.tensor_scalar_max(wst[0], bc[0], 0.25),
               after=[t_ob0, t_om])
    t_obn = vop(lambda h: h.tensor_scalar_min(bc[0], bc[0], -0.25), after=[t_oa])
    t_oc1 = vop(lambda h: h.tensor_sub(wst[0], wst[0], bc[0]), after=[t_obn])
    t_oc2 = vop(lambda h: h.tensor_mul(wst[0], wst[0], bc[2]), after=[t_oc1])
    t_ga1 = vop(lambda h: h.tensor_add(bc[0], bc[0], wst[0]), after=[t_oc2])
    t_ob3 = bcast("scalar", bc[3], bout[0:1, :], after=sc_done + [t_gbe])

    x3_store_toks = []
    prev_wo = [ptq_free[0], ptq_free[1]]
    prev_x3 = {}
    prev_evo = [sc_done[-1]]
    for n in range(4):
        for tg in range(2):
            mm_group = []
            for dc in range(DC):
                bi = dc % 2
                tw = b.d("gpsimd",
                         lambda h, dc=dc, n=n, bi=bi: [h.dma_start(
                             out=wobuf[bi],
                             in_=WoutT[dc * 128:(dc + 1) * 128,
                                       n * 512:(n + 1) * 512])],
                         1, after=[prev_wo[bi]])
                def mkM(dc=dc, tg=tg, bi=bi):
                    def f(h):
                        last = None
                        for j in range(4):
                            t = tg * 4 + j
                            last = h.matmul(
                                pb[j].ap,
                                ctxf[:, dc * TOK + t * 128:dc * TOK + (t + 1) * 128],
                                wobuf[bi],
                                start=(dc == 0), stop=(dc == DC - 1))
                        return last
                    return f
                mmt = top(mkM(), after=[tw] + ctxf_done +
                          [pb[j].rtok for j in range(4)])
                prev_wo[bi] = mmt
                mm_group.append(mmt)
            for j in range(4):
                pb[j].wtok = mm_group[-1]
            for j in range(4):
                t = tg * 4 + j
                bkj = pb[j]
                lx = dma("scalar", tm0[:, j * 512:(j + 1) * 512],
                         xres[t * 128:(t + 1) * 128, n * 512:(n + 1) * 512],
                         after=[tm0_free[0], prev_x3.get((t, n), None)] +
                               x2_store_toks)
                e1 = vop(lambda h, bkj=bkj, n=n, t=t, j=j: h.scalar_tensor_tensor(
                    tm1[:, j * 512:(j + 1) * 512], bkj.ap, lrec[:, t:t + 1],
                    bc[3][:, n * 512:(n + 1) * 512], OP.mult, OP.add),
                    after=[bkj.wtok, t_ob3, t_lrec, prev_evo[0], tm1_free[0]])
                e2 = vop(lambda h, n=n, j=j: h.tensor_tensor(
                    tm1[:, j * 512:(j + 1) * 512], tm1[:, j * 512:(j + 1) * 512],
                    bc[0][:, n * 512:(n + 1) * 512], OP.mult),
                    after=[e1, t_ga1])
                e3 = vop(lambda h, j=j: h.tensor_add(
                    tm1[:, j * 512:(j + 1) * 512], tm1[:, j * 512:(j + 1) * 512],
                    tm0[:, j * 512:(j + 1) * 512]), after=[e2, lx])
                bkj.rtok = e3
                tm0_free[0] = e3
                st3 = dma("sync", xres[t * 128:(t + 1) * 128, n * 512:(n + 1) * 512],
                          tm1[:, j * 512:(j + 1) * 512], after=[e3])
                x3_store_toks.append(st3)
                prev_x3[(t, n)] = st3
                prev_evo[0] = st3
    tm1_free[:] = [prev_evo[0]]

    if nphase < 6:
        return _finalize(nc, b, taps, y_out)

    moe_gate = x3_store_toks + ctxf_done
    t_mb1 = bcast("scalar", bc[1], modL[1][0:1, 4 * D:5 * D],
                  after=[t_mod[1]] + moe_gate)
    t_mb1 = vop(lambda h: h.tensor_scalar_add(bc[1], bc[1], 1.0), after=[t_mb1])
    t_mb2 = bcast("scalar", bc[2], modL[1][0:1, 3 * D:4 * D],
                  after=[t_mod[1]] + moe_gate)

    h3_done = []
    rt_done = []
    logit_done = []
    for t in range(NT):
        tl = dma("sync", tm0, xres[t * 128:(t + 1) * 128, :],
                 after=[tm0_free[0]] + x3_store_toks)
        tm0_free[0] = tl
        cb2 = 32 * (t % 2)
        st_ = lambda c, cb2=cb2: statw[:, cb2 + c:cb2 + c + 1]
        guard = list(statg[t % 2])
        statg[t % 2] = []
        t1 = vop(lambda h, st_=st_: h.reduce_sum(st_(1), tm0, axis=AX),
                 after=[tl] + guard)
        t2 = sop(lambda h: h.activation(tm1, tm0, AF.Square), after=[tl] + tm1_free)
        t3 = vop(lambda h, st_=st_: h.reduce_sum(st_(2), tm1, axis=AX), after=[t2])
        t4 = vop(lambda h, st_=st_: h.tensor_scalar_mul(st_(1), st_(1),
                                                        float(1.0 / D)), after=[t1])
        t5 = vop(lambda h, st_=st_: h.tensor_mul(st_(3), st_(1), st_(1)), after=[t4])
        t6 = vop(lambda h, st_=st_: h.tensor_scalar_mul(st_(2), st_(2),
                                                        float(1.0 / D)), after=[t3])
        t7 = vop(lambda h, st_=st_: h.tensor_sub(st_(2), st_(2), st_(3)),
                 after=[t5, t6])
        t8 = sop(lambda h, st_=st_: h.activation(st_(4), st_(2), AF.Sqrt,
                                                 bias=cvec[:, 0:1]),
                 after=[t7, t_cvec])
        t9 = vop(lambda h, st_=st_: h.reciprocal(st_(5), st_(4)), after=[t8])
        ta = vop(lambda h, st_=st_: h.tensor_scalar(tm1, tm0, st_(1), st_(5),
                                                    OP.subtract, OP.mult),
                 after=[t9, t3])
        tb = vop(lambda h: h.tensor_tensor(tm1, tm1, bc[1], OP.mult),
                 after=[ta, t_mb1])
        tc = vop(lambda h: h.tensor_tensor(tm1, tm1, bc[2], OP.add),
                 after=[tb, t_mb2])
        statg[t % 2].append(t8)
        tm0_free[0] = ta
        lg_mm = None
        for dc in range(DC):
            bk = pb[4 + dc % 2]
            bk.wtok = top(lambda h, bk=bk, dc=dc: h.transpose(
                bk.ap[:, 0:128], tm1[:, dc * 128:(dc + 1) * 128], id32),
                after=[tc, bk.rtok, t_id32])
            ef = vop(lambda h, bk=bk: h.tensor_copy(hfm32, bk.ap[:, 0:128]),
                     after=[bk.wtok, lg_mm])
            eb = vop(lambda h, bk=bk, dc=dc, t=t: h.tensor_copy(
                h3fm[:, dc * TOK + t * 128:dc * TOK + (t + 1) * 128],
                bk.ap[:, 0:128]),
                after=[bk.wtok] + (moe_gate if t == 0 and dc == 0 else []))
            bk.rtok = eb
            h3_done.append(eb)
            lg_mm = top(lambda h, dc=dc: h.matmul(
                pb[2].ap[:, 0:8], hfm32, gateTs[:, dc * 8:(dc + 1) * 8],
                start=(dc == 0), stop=(dc == DC - 1)),
                after=[ef, t_gateT, pb[2].rtok])
        pb[2].wtok = lg_mm
        r0 = vop(lambda h: h.tensor_copy(rt[:, 0:8], pb[2].ap[:, 0:8]),
                 after=[lg_mm] + rt_done)
        pb[2].rtok = r0
        r1 = vop(lambda h: h.reduce_max(rt[:, 8:9], rt[:, 0:8], axis=AX), after=[r0])
        r2 = vop(lambda h: h.tensor_scalar_mul(rt[:, 8:9], rt[:, 8:9], -1.0),
                 after=[r1])
        r3 = sop(lambda h: h.activation(rt[:, 0:8], rt[:, 0:8], AF.Exp,
                                        bias=rt[:, 8:9]), after=[r2])
        r4 = vop(lambda h: h.reduce_sum(rt[:, 9:10], rt[:, 0:8], axis=AX),
                 after=[r3])
        r5 = vop(lambda h: h.reciprocal(rt[:, 10:11], rt[:, 9:10]), after=[r4])
        r6 = vop(lambda h: h.tensor_scalar(rt[:, 0:8], rt[:, 0:8], rt[:, 10:11],
                                           None, OP.mult), after=[r5])
        g1_ = vop(lambda h: h.tensor_tensor(rt2[:, 0:4], rt[:, 0:8:2],
                                            rt[:, 1:8:2], OP.max), after=[r6])
        g2_ = vop(lambda h: h.reduce_max(rt2[:, 4:5], rt2[:, 0:4], axis=AX),
                  after=[g1_])
        g3_ = vop(lambda h: h.tensor_scalar(rt2[:, 5:9], rt2[:, 0:4], rt2[:, 4:5],
                                            None, OP.is_ge), after=[g2_])
        g4_ = vop(lambda h: h.tensor_scalar(rt2[:, 5:9], rt2[:, 5:9], -1.0, -1.0,
                                            OP.mult, OP.subtract), after=[g3_])
        g5_ = vop(lambda h: h.tensor_tensor(rt2[:, 5:9], rt2[:, 0:4], rt2[:, 5:9],
                                            OP.mult), after=[g4_])
        g6_ = vop(lambda h: h.reduce_max(rt2[:, 9:10], rt2[:, 5:9], axis=AX),
                  after=[g5_])
        g7_ = vop(lambda h: h.tensor_scalar(rt2[:, 10:14], rt2[:, 0:4], rt2[:, 9:10],
                                            None, OP.is_ge), after=[g6_])
        m1_ = vop(lambda h: h.tensor_tensor(rt[:, 0:8:2], rt[:, 0:8:2],
                                            rt2[:, 10:14], OP.mult), after=[g7_])
        m2_ = vop(lambda h: h.tensor_tensor(rt[:, 1:8:2], rt[:, 1:8:2],
                                            rt2[:, 10:14], OP.mult), after=[m1_])
        n1 = vop(lambda h: h.reduce_max(rt[:, 8:9], rt[:, 0:8], axis=AX),
                 after=[m2_])
        n2 = vop(lambda h: h.tensor_scalar(rt[:, 10:11] if False else rt2[:, 0:8],
                                           rt[:, 0:8], rt[:, 8:9], None, OP.is_ge),
                 after=[n1])
        n3 = vop(lambda h: h.tensor_scalar(rt2[:, 0:8], rt2[:, 0:8], -1.0, -1.0,
                                           OP.mult, OP.subtract), after=[n2])
        n4 = vop(lambda h: h.tensor_tensor(rt2[:, 0:8], rt[:, 0:8], rt2[:, 0:8],
                                           OP.mult), after=[n3])
        n5 = vop(lambda h: h.reduce_max(rt[:, 9:10], rt2[:, 0:8], axis=AX),
                 after=[n4])
        n6 = vop(lambda h: h.tensor_scalar(rt2[:, 0:8], rt[:, 0:8], rt[:, 9:10],
                                           None, OP.is_ge), after=[n5])
        n7 = vop(lambda h, t=t: h.tensor_tensor(wmat_sb[:, t * 8:(t + 1) * 8],
                                                rt[:, 0:8], rt2[:, 0:8], OP.mult),
                 after=[n6])
        rt_done = [n7]
        logit_done.append(n7)

    acc_z = []
    for t in range(NT):
        acc_z.append(vop(lambda h, t=t: h.memset(acc[t], 0.0),
                         after=ctxf_done + [prev_wo[0], prev_wo[1]] + x3_store_toks))

    gu_free = [sc_done[-1]]
    prev_gw = [h3_done[-1], h3_done[-1]]
    prev_dw = [prev_wo[0], prev_wo[1]]
    acc_done = list(acc_z)
    silu_strip_free = [tm0_free[0]]
    for ee in range(10):
        is_sh = ee >= 8
        shh = ee - 8
        NF = 11
        gu_toks = []
        for fb in range(NF):
            bi = fb % 2
            if is_sh:
                gsrc = sgWT[:, (shh * 11 + fb) * 128:(shh * 11 + fb + 1) * 128]
                usrc = suWT[:, (shh * 11 + fb) * 128:(shh * 11 + fb + 1) * 128]
            else:
                gsrc = gWT[ee, :, fb * 128:(fb + 1) * 128]
                usrc = uWT[ee, :, fb * 128:(fb + 1) * 128]
            twg = b.d("gpsimd", lambda h, gsrc=gsrc, bi=bi: [h.dma_start(
                out=gwb[bi].rearrange("p (c q) -> p c q", q=128),
                in_=gsrc.rearrange("(c p) q -> p c q", p=128))],
                1, after=[prev_gw[bi]])
            twu = b.d("gpsimd", lambda h, usrc=usrc, bi=bi: [h.dma_start(
                out=uwb[bi].rearrange("p (c q) -> p c q", q=128),
                in_=usrc.rearrange("(c p) q -> p c q", p=128))],
                1, after=[prev_gw[bi]])
            mms = []
            for n in range(2):
                gbk, ubk = pb[0 + n], pb[2 + n]
                def mkG(bi=bi, n=n):
                    def f(h):
                        last = None
                        for dc in range(DC):
                            last = h.matmul(
                                pb[0 + n].ap, gwb[bi][:, dc * 128:(dc + 1) * 128],
                                h3fm[:, dc * TOK + n * 512:dc * TOK + (n + 1) * 512],
                                start=(dc == 0), stop=(dc == DC - 1))
                        return last
                    return f
                def mkU(bi=bi, n=n):
                    def f(h):
                        last = None
                        for dc in range(DC):
                            last = h.matmul(
                                pb[2 + n].ap, uwb[bi][:, dc * 128:(dc + 1) * 128],
                                h3fm[:, dc * TOK + n * 512:dc * TOK + (n + 1) * 512],
                                start=(dc == 0), stop=(dc == DC - 1))
                        return last
                    return f
                gbk.wtok = top(mkG(), after=[twg, gbk.rtok] + h3_done)
                ubk.wtok = top(mkU(), after=[twu, ubk.rtok] + h3_done)
                mms += [gbk.wtok, ubk.wtok]
            prev_gw[bi] = mms[-1]
            for n in range(2):
                gbk, ubk = pb[0 + n], pb[2 + n]
                s1_ = sop(lambda h, gbk=gbk, n=n: h.activation(
                    tm0[:, n * 512:(n + 1) * 512], gbk.ap, AF.Silu),
                    after=[gbk.wtok, silu_strip_free[0]])
                gbk.rtok = s1_
                gu1 = vop(lambda h, ubk=ubk, fb=fb, n=n: h.tensor_tensor(
                    gu_e[:, fb * TOK + n * 512:fb * TOK + (n + 1) * 512],
                    tm0[:, n * 512:(n + 1) * 512], ubk.ap, OP.mult),
                    after=[s1_, ubk.wtok, gu_free[0]])
                ubk.rtok = gu1
                gu_toks.append(gu1)
            silu_strip_free[0] = gu_toks[-1]
        ph2 = []
        for n in range(4):
            for tg in range(2):
                mmds = []
                for fb in range(NF):
                    bi = fb % 2
                    dsrc = (sdWT[(shh * 11 + fb) * 128:(shh * 11 + fb + 1) * 128,
                                 n * 512:(n + 1) * 512] if is_sh else
                            dWT[ee, fb * 128:(fb + 1) * 128, n * 512:(n + 1) * 512])
                    twd = b.d("gpsimd", lambda h, dsrc=dsrc, bi=bi: [h.dma_start(
                        out=dwb[bi], in_=dsrc)], 1, after=[prev_dw[bi]])
                    def mkD(fb=fb, tg=tg, bi=bi):
                        def f(h):
                            last = None
                            for j in range(4):
                                t = tg * 4 + j
                                last = h.matmul(
                                    pb[j].ap,
                                    gu_e[:, fb * TOK + t * 128:fb * TOK + (t + 1) * 128],
                                    dwb[bi],
                                    start=(fb == 0), stop=(fb == NF - 1))
                            return last
                        return f
                    mmd = top(mkD(), after=[twd] + gu_toks +
                              [pb[j].rtok for j in range(4)])
                    prev_dw[bi] = mmd
                    mmds.append(mmd)
                for j in range(4):
                    pb[j].wtok = mmds[-1]
                for j in range(4):
                    t = tg * 4 + j
                    bkj = pb[j]
                    if is_sh:
                        ea = vop(lambda h, bkj=bkj, t=t, n=n: h.tensor_tensor(
                            acc[t][:, n * 512:(n + 1) * 512],
                            acc[t][:, n * 512:(n + 1) * 512], bkj.ap, OP.add),
                            after=[bkj.wtok] + acc_done)
                    else:
                        ea = vop(lambda h, bkj=bkj, t=t, n=n, ee=ee:
                                 h.scalar_tensor_tensor(
                                     acc[t][:, n * 512:(n + 1) * 512], bkj.ap,
                                     wmat_sb[:, t * 8 + ee:t * 8 + ee + 1],
                                     acc[t][:, n * 512:(n + 1) * 512],
                                     OP.mult, OP.add),
                                 after=[bkj.wtok] + acc_done + logit_done)
                    bkj.rtok = ea
                    ph2.append(ea)
        acc_done = ph2[-8:]
        gu_free[0] = ph2[-1]

    t_yb0 = bcast("scalar", bc[0], modL[1][0:1, 5 * D:6 * D],
                  after=[t_mod[1], prev_dw[0], prev_dw[1]] + acc_done)
    t_ym = vop(lambda h: h.tensor_scalar(bc[2], bc[0], 0.0, None, OP.is_ge),
               after=[t_yb0] + acc_done)
    t_ya = vop(lambda h: h.tensor_scalar_max(wst[0], bc[0], 0.25),
               after=[t_yb0, t_ym])
    t_ybn = vop(lambda h: h.tensor_scalar_min(bc[0], bc[0], -0.25), after=[t_ya])
    t_yc1 = vop(lambda h: h.tensor_sub(wst[0], wst[0], bc[0]), after=[t_ybn])
    t_yc2 = vop(lambda h: h.tensor_mul(wst[0], wst[0], bc[2]), after=[t_yc1])
    t_gb1 = vop(lambda h: h.tensor_add(bc[0], bc[0], wst[0]), after=[t_yc2])
    y_toks = []
    prev_y = [None]
    tm0_free[0] = silu_strip_free[0]
    for t in range(NT):
        lx = dma("scalar", tm0, xres[t * 128:(t + 1) * 128, :],
                 after=[tm0_free[0]] + x3_store_toks)
        y1 = vop(lambda h, t=t: h.tensor_tensor(tm1, acc[t], bc[0], OP.mult),
                 after=[t_gb1] + acc_done + tm1_free + [prev_y[0]])
        y2 = vop(lambda h: h.tensor_add(tm1, tm1, tm0), after=[y1, lx])
        tm0_free[0] = y2
        tm1_free[:] = [y2]
        yst = dma("sync", y_out[t * 128:(t + 1) * 128, :], tm1, after=[y2])
        y_toks.append(yst)
        prev_y[0] = yst

    return _finalize(nc, b, taps, y_out)


def _finalize(nc, b, taps, y_out):
    allt = [(s, v) for s, v in b.cnt.items()]
    tap_toks = []
    for name, (src, dst) in taps.items():
        tt = b.d("sync", lambda h, s=src, d=dst: [h.dma_start(out=d, in_=s)],
                 1, after=list(allt))
        tap_toks.append(tt)
    fin = [(s, v) for s, v in b.cnt.items()]
    b.c("sync", lambda h: None, after=fin)
    _emit(nc, b)
    return nc


def _prep_inputs(inputs):
    f32 = np.float32
    x = np.ascontiguousarray(np.asarray(inputs["L0_attn_hidden_states"],
                                        dtype=f32).reshape(8192, 2048))
    c01 = np.concatenate([np.asarray(inputs["L0_c"], f32).reshape(1, 2048),
                          np.asarray(inputs["L1_c"], f32).reshape(1, 2048)], axis=1)
    wmod0_full = np.asarray(inputs["L0_wmod"], f32)
    wmod1_full = np.asarray(inputs["L1_wmod"], f32)

    common = dict(
        WiT=np.ascontiguousarray(np.asarray(inputs["L0_attn_Wi"], f32).T),
        WfT=np.ascontiguousarray(np.asarray(inputs["L0_attn_Wf"], f32).T),
        WgT=np.ascontiguousarray(np.asarray(inputs["L0_attn_Wg"], f32).T),
        WoT=np.ascontiguousarray(np.asarray(inputs["L0_attn_Wo"], f32).T),
        W8T=np.ascontiguousarray(np.asarray(inputs["L0_ffn_weight"]).T),
        nrm_i=np.asarray(inputs["L0_attn_norm_i"], f32).reshape(1, 2048),
        nrm_f=np.asarray(inputs["L0_attn_norm_f"], f32).reshape(1, 2048),
        nrm_g=np.asarray(inputs["L0_attn_norm_g"], f32).reshape(1, 2048),
        nrm_o=np.asarray(inputs["L0_attn_norm_o"], f32).reshape(1, 2048),
        gnw=np.asarray(inputs["L0_attn_g_norm_w"], f32).reshape(1, 2048),
        w8s=np.asarray(inputs["L0_ffn_weight_scale"], f32).reshape(1, 1),
        in_s=np.asarray(inputs["L0_ffn_input_scale"]).reshape(1, 1),
        c01=c01,
        WqkvT=np.ascontiguousarray(np.asarray(inputs["L1_attn_w_qkv"], f32).T),
        bqkv=np.asarray(inputs["L1_attn_b_qkv"], f32).reshape(1, 6144),
        WoutT=np.ascontiguousarray(np.asarray(inputs["L1_attn_w_out"], f32).T),
        bout=np.asarray(inputs["L1_attn_b_out"], f32).reshape(1, 2048),
        gateT=np.ascontiguousarray(np.asarray(inputs["L1_ffn_gate_weight"], f32).T),
        gWT=np.ascontiguousarray(
            np.asarray(inputs["L1_ffn_gate_w"], f32).transpose(0, 2, 1)),
        uWT=np.ascontiguousarray(
            np.asarray(inputs["L1_ffn_up_w"], f32).transpose(0, 2, 1)),
        dWT=np.ascontiguousarray(
            np.asarray(inputs["L1_ffn_down_w"], f32).transpose(0, 2, 1)),
        sgWT=np.ascontiguousarray(np.asarray(inputs["L1_ffn_sh_gate_w"], f32).T),
        suWT=np.ascontiguousarray(np.asarray(inputs["L1_ffn_sh_up_w"], f32).T),
        sdWT=np.ascontiguousarray(np.asarray(inputs["L1_ffn_sh_down_w"], f32).T),
        id32=np.eye(128, dtype=f32),
        cvec=np.array([[1e-5, 1e-8, 0, 0, 0, 0, 0, 0]], dtype=f32),
        idbf=np.eye(128, dtype=ml_dtypes.bfloat16),
    )
    in_maps = []
    for r in range(NCORE):
        m = dict(common)
        m["x"] = np.ascontiguousarray(x[r * TOK:(r + 1) * TOK])
        m["wmod0"] = np.ascontiguousarray(wmod0_full[:, r * 1536:(r + 1) * 1536])
        m["wmod1"] = np.ascontiguousarray(wmod1_full[:, r * 1536:(r + 1) * 1536])
        rm = np.zeros((1, 8), f32)
        rm[0, :r] = 1.0
        m["rmask"] = rm
        in_maps.append(m)
    return in_maps


def kernel(**inputs):
    in_maps = _prep_inputs(inputs)
    nc = build_nc()
    res = run_bass_kernel_spmd(nc, in_maps, core_ids=list(range(NCORE)))
    y = np.concatenate([np.asarray(r["y"]) for r in res.results], axis=0)
    return y.reshape(1, 8192, 2048).astype(np.float32)


# version 07 (final)
# speedup vs baseline: 1.2087x; 1.2087x over previous
import math
import numpy as np
import ml_dtypes
import concourse.bass as bass
import concourse.mybir as mybir
from concourse.bass_utils import run_bass_kernel_spmd

F32 = mybir.dt.float32
BF16 = mybir.dt.bfloat16
FP8 = mybir.dt.float8e4
AF = mybir.ActivationFunctionType
OP = mybir.AluOpType
AX = mybir.AxisListType.X

NCORE = 8
TOK = 1024
D = 2048
DC = D // 128
NT = TOK // 128
MAGIC = float(3 * 2.0 ** 22)
ISQD = float(1.0 / math.sqrt(2048.0))

ENGINES = ("sync", "tensor", "vector", "scalar", "gpsimd")
_TAPS = ()
_NPHASE = 99


def _norm_waits(after):
    m = {}
    for t in after:
        if t is None:
            continue
        s, v = t
        if v > m.get(s, 0):
            m[s] = v
    return tuple(m.items())


QUEUE_SEMS = {"sync": [f"dS{i}" for i in range(6)],
              "scalar": [f"dA{i}" for i in range(4)],
              "gpsimd": [f"dG{i}" for i in range(6)]}


class B:

    def __init__(self):
        self.q = {e: [] for e in ENGINES}
        self.cnt = {}
        self.qrot = {e: 0 for e in QUEUE_SEMS}
        self.serial = False

    def _bump(self, sem, n):
        self.cnt[sem] = self.cnt.get(sem, 0) + n
        return (sem, self.cnt[sem])

    def c(self, eng, fn, after=(), sem=None, n=1):
        if self.serial:
            after = list(after) + [(s, v) for s, v in self.cnt.items()]
        tok = self._bump(sem, n) if sem else None
        self.q[eng].append(("c", _norm_waits(after), fn, sem, n))
        return tok

    def d(self, eng, fn, ndma=1, after=(), sem=None):
        if self.serial:
            after = list(after) + [(s, v) for s, v in self.cnt.items()]
        sems = QUEUE_SEMS[eng]
        i = self.qrot[eng]
        self.qrot[eng] = (i + 1) % len(sems)
        sem = sems[i]
        gate = (sem, self.cnt.get(sem, 0))
        tok = self._bump(sem, 16 * ndma)
        self.q[eng].append(("d", _norm_waits(list(after) + [gate]), fn, sem, ndma))
        return tok


class Bank:
    def __init__(self, ap):
        self.ap = ap
        self.wtok = None
        self.rtok = None


def _check_schedule(b):
    pos = {e: 0 for e in ENGINES}
    cnt = {s: 0 for s in b.cnt}
    progress = True
    while progress:
        progress = False
        for e in ENGINES:
            while pos[e] < len(b.q[e]):
                kind, waits, fn, sem, n = b.q[e][pos[e]]
                if any(cnt[s] < v for s, v in waits):
                    break
                if sem is not None:
                    cnt[sem] += (16 * n) if kind == "d" else n
                pos[e] += 1
                progress = True
    stuck = {e: (pos[e], len(b.q[e])) for e in ENGINES if pos[e] < len(b.q[e])}
    if stuck:
        detail = []
        for e, (p, tot) in stuck.items():
            kind, waits, fn, sem, n = b.q[e][p]
            blocked = [(s, v, cnt[s]) for s, v in waits if cnt[s] < v]
            detail.append(f"{e}[{p}/{tot}] blocked on {blocked}")
        raise RuntimeError("schedule deadlock:\n" + "\n".join(detail))


def _emit(nc, b):
    _check_schedule(b)
    from contextlib import ExitStack
    with ExitStack() as es:
        sems = {name: es.enter_context(nc.semaphore(name)) for name in sorted(b.cnt)}

        def mk(eng_name):
            def run(h):
                last_wait = {}
                for kind, waits, fn, sem, n in b.q[eng_name]:
                    for s, v in waits:
                        if last_wait.get(s, -1) >= v:
                            continue
                        h.wait_ge(sems[s], v)
                        last_wait[s] = v
                    if kind == "c":
                        last = fn(h)
                        if sem is not None:
                            assert last is not None
                            last.then_inc(sems[sem], n)
                    else:
                        instrs = fn(h)
                        assert len(instrs) == n, (eng_name, len(instrs), n)
                        for ins in instrs:
                            ins.then_inc(sems[sem], 16)
            return run

        runners = {e: mk(e) for e in ENGINES}
        with nc.Block() as block:
            @block.sync
            def _(h):
                runners["sync"](h)

            @block.tensor
            def _(h):
                runners["tensor"](h)

            @block.vector
            def _(h):
                runners["vector"](h)

            @block.scalar
            def _(h):
                runners["scalar"](h)

            @block.gpsimd
            def _(h):
                runners["gpsimd"](h)


def build_nc(nphase=None):
    if nphase is None:
        nphase = _NPHASE
    nc = bass.Bass("TRN2", target_bir_lowering=False, debug=False)
    b = B()

    def inp(name, shape, dt=F32):
        return nc.dram_tensor(name, list(shape), dt, kind="ExternalInput").ap()

    x_in = inp("x", (TOK, D))
    WT = {w: inp(f"W{w}T", (D, D)) for w in "ifgo"}
    nrm = {w: inp(f"nrm_{w}", (1, D)) for w in "ifgo"}
    gnw_in = inp("gnw", (1, D))
    W8T = inp("W8T", (D, D), FP8)
    w8s_in = inp("w8s", (1, 1))
    ins_in = inp("in_s", (1, 1), BF16)
    c01 = inp("c01", (1, 2 * D))
    wmod = [inp("wmod0", (D, 1536)), inp("wmod1", (D, 1536))]
    WqkvT = inp("WqkvT", (D, 3 * D))
    bqkv = inp("bqkv", (1, 3 * D))
    WoutT = inp("WoutT", (D, D))
    bout = inp("bout", (1, D))
    gateT = inp("gateT", (D, 8))
    gWT = inp("gWT", (8, D, 1408))
    uWT = inp("uWT", (8, D, 1408))
    dWT = inp("dWT", (8, 1408, D))
    sgWT = inp("sgWT", (D, 2816))
    suWT = inp("suWT", (D, 2816))
    sdWT = inp("sdWT", (2816, D))
    id32_in = inp("id32", (128, 128))
    cvec_in = inp("cvec", (1, 8))
    idbf_in = inp("idbf", (128, 128), BF16)
    rmask_in = inp("rmask", (1, 8))

    y_out = nc.dram_tensor("y", [TOK, D], F32, kind="ExternalOutput").ap()

    def scr(name, shape, dt=F32):
        return nc.dram_tensor(name, list(shape), dt, kind="Internal").ap()

    cc_mod_in = scr("cc_mod_in", (1, 3072))
    cc_mod_out = scr("cc_mod_out", (8, 3072))
    modL = [scr("modL0", (1, 6 * D)), scr("modL1", (1, 6 * D))]
    cc_hg_in = scr("cc_hg_in", (1, 2 * D))
    cc_hg_out = scr("cc_hg_out", (8, 2 * D))
    scr_f = scr("scr_f", (D, TOK))
    scr_i = scr("scr_i", (D, TOK))
    scr_g = scr("scr_g", (TOK, D))
    scr_h = scr("scr_h", (TOK, D))
    xres = scr("xres", (TOK, D))
    srec_d = scr("srec_d", (4, TOK))
    smallv = scr("smallv", (1, 64))
    KVC1 = D * TOK + 32
    cc_k_in = scr("cc_k_in", (1, KVC1), BF16)
    cc_k_out = scr("cc_k_out", (8, KVC1), BF16)
    ccv_i = scr("ccv_i", (1, TOK * D), BF16)
    ccv_o = scr("ccv_o", (8, TOK * D), BF16)
    qn_d = scr("qn_d", (1, TOK))
    dbg1 = scr("dbg1", (TOK, D))
    dbg2 = scr("dbg2", (TOK, D))
    dbg3 = scr("dbg3", (128, DC * TOK), BF16)
    dbg4 = scr("dbg4", (128, D))
    dbg5 = scr("dbg5", (1, 64))
    dbg6 = scr("dbg6", (D, TOK))
    dbg7 = scr("dbg7", (D, TOK))

    taps = {}
    for name in _TAPS:
        src = {"scr_f": scr_f, "scr_i": scr_i, "scr_g": scr_g, "scr_h": scr_h,
               "xres": xres, "modL0": modL[0], "modL1": modL[1],
               "srec_d": srec_d, "qn_d": qn_d, "dbg1": dbg1, "dbg2": dbg2,
               "dbg3": dbg3, "dbg4": dbg4, "dbg5": dbg5, "dbg6": dbg6,
               "dbg7": dbg7}[name]
        taps[name] = (src, nc.dram_tensor("tap_" + name, list(src.tensor.shape),
                                          src.dtype, kind="ExternalOutput").ap())

    KB = 1024
    arena_h = nc.alloc_sbuf_tensor("arena", [128, 178 * KB], mybir.dt.uint8)
    base = nc.lookup_mloc(arena_h).addr

    def at(name, off, shape, dt):
        return nc.alloc_sbuf_tensor_at(name, list(shape), dt, offset=base + off).ap()

    OFF_A, OFF_B, OFF_C, OFF_P = 0, 32 * KB, 64 * KB, 96 * KB
    OFF_T0, OFF_T1 = 136 * KB, 144 * KB
    OFF_W0, OFF_W1 = 152 * KB, 160 * KB
    OFF_S = 168 * KB

    kq_i = at("kq_i", OFF_A, (128, DC * TOK), BF16)
    kq_f = at("kq_f", OFF_B, (128, DC * TOK), BF16)
    kq_g = at("kq_g", OFF_A, (128, DC * TOK), BF16)
    kq_o = at("kq_o", OFF_A, (128, DC * TOK), BF16)
    q8T = at("q8T", OFF_A, (128, DC * TOK), BF16)
    wmodb = [at(f"wmodb{i}", OFF_A + i * 6 * KB, (128, 1536), F32) for i in range(2)]
    sreb = [at(f"sreb{i}", OFF_C + i * 4 * KB, (128, TOK), F32) for i in range(3)]
    scanio = at("scanio", OFF_C + 8 * KB, (128, 2 * TOK), F32)
    scano = at("scano", OFF_C + 16 * KB, (128, TOK), F32)
    ev = [at(f"ev{i}", OFF_C + 20 * KB + i * 4 * KB, (128, TOK), F32) for i in range(3)]
    sreb_g = at("sreb_g", OFF_B, (128, TOK), F32)
    ev2 = [at(f"ev2{i}", OFF_B + 4 * KB + i * 4 * KB, (128, TOK), F32) for i in range(2)]
    gtile = at("gtile", OFF_B + 12 * KB, (128, D), F32)
    htile = at("htile", OFF_B + 20 * KB, (128, D), F32)
    Qfm = at("Qfm", OFF_A, (128, DC * TOK), BF16)
    hfm = at("hfm", OFF_B, (128, DC * TOK), BF16)
    ctx = [at(f"ctx{t}", (OFF_B if t < 4 else OFF_C) + (t % 4) * 8 * KB, (128, D), F32)
           for t in range(NT)]
    acc = [at(f"acc{t}", (OFF_A if t < 4 else OFF_B) + (t % 4) * 8 * KB, (128, D), F32)
           for t in range(NT)]
    h3fm = at("h3fm", OFF_C, (128, DC * TOK), BF16)

    bc = [at(f"bc{i}", OFF_P + i * 8 * KB, (128, D), F32) for i in range(4)]
    mq = [at(f"mq{i}", OFF_P + 32 * KB + i * 4 * KB, (128, DC * 128), BF16)
          for i in range(2)]
    kbuf = [at(f"kbuf{i}", OFF_P + i * 8 * KB, (128, DC * 256), BF16) for i in range(2)]
    vbuf = [at(f"vbuf{i}", OFF_P + 16 * KB + i * 8 * KB, (128, 2 * D), BF16)
            for i in range(2)]
    ptq = [at(f"ptq{i}", OFF_P + 32 * KB + i * 2 * KB, (128, 8 * 128), BF16)
           for i in range(2)]
    pP = [at(f"pP{i}", OFF_P + 36 * KB + i * KB, (128, 512), BF16) for i in range(2)]
    wobuf = [at(f"wobuf{i}", OFF_P + 38 * KB + i * KB, (128, 512), BF16)
             for i in range(2)]
    gu_e = at("gu_e", OFF_P, (128, 11 * TOK), BF16)
    gwb = [at(f"gwb{i}", OFF_P + 22 * KB + i * 4 * KB, (128, DC * 128), BF16)
           for i in range(2)]
    uwb = [at(f"uwb{i}", OFF_P + 30 * KB + i * 4 * KB, (128, DC * 128), BF16)
           for i in range(2)]
    dwb = [at(f"dwb{i}", OFF_P + 38 * KB + i * KB, (128, 512), BF16) for i in range(2)]
    w8b = [at(f"w8b{i}", OFF_P + 38 * KB + i * KB, (128, 512), BF16) for i in range(2)]
    w8big = [at("w8big0", OFF_P + 32 * KB, (128, 8 * 512), BF16),
             at("w8big1", OFF_C, (128, 8 * 512), BF16)]
    wobig = [at(f"wobig{i}", OFF_P + 8 * KB + i * 8 * KB, (128, 8 * 512), BF16)
             for i in range(2)]
    dwn = [at(f"dwn{i}", OFF_T0 if i == 0 else OFF_W0, (128, 11 * 512), BF16)
           for i in range(2)]
    tm0 = at("tm0", OFF_T0, (128, D), F32)
    tm1 = at("tm1", OFF_T1, (128, D), F32)
    wst = [at(f"wst{i}", OFF_W0 if i == 0 else OFF_W1, (128, D), F32) for i in range(2)]

    sm_off = [OFF_S]

    def small(name, shape, dt=F32):
        sz = int(np.prod(shape[1:])) * {F32: 4, BF16: 2, FP8: 1}[dt]
        sz = (sz + 31) // 32 * 32
        ap = at(name, sm_off[0], shape, dt)
        sm_off[0] += sz
        return ap

    id32 = small("id32s", (128, 128))
    statw = small("statws", (128, 64))
    idbf = small("idbfs", (128, 128), BF16)
    silc = small("silcs", (128, 2 * DC))
    mod_sb = at("mod_sbs", OFF_B, (1, 3072), F32)
    stat = small("stats", (128, 32))
    swv = small("swvs", (128, 8))
    carry = small("carrys", (128, DC))
    facc = small("faccs", (128, DC))
    hcar = small("hcars", (128, DC))
    lnf = small("lnfs", (128, DC))
    rmk = small("rmks", (128, 8))
    ghF = small("ghFs", (128, DC))
    ghH = small("ghHs", (128, DC))
    gtmp = small("gtmps", (128, DC))
    wab = small("wabs", (128, 8))
    onf = small("onfs", (128, 1))
    sc1 = small("sc1s", (1, 64))
    srow = small("srows", (1, 128))
    c128 = small("c128s", (128, 8))
    qn_sb = small("qn_sbs", (128, NT))
    biasB = small("biasBs", (128, NT))
    lpart = small("lparts", (128, 32))
    lrec = small("lrecs", (128, NT))
    gateTs = small("gateTss", (128, DC * 8))
    wmat_sb = small("wmat_sbs", (128, NT * 8))
    hfm32 = small("hfm32s", (128, 128))
    scn2 = small("scn2s", (128, 256))
    scn2o = small("scn2os", (128, 128))
    bq_sb = small("bq_sbs", (128, 48))
    cvec = small("cvecs", (128, 8))
    rt = small("rts", (128, 16))
    rt2 = small("rt2s", (128, 16))
    oinv = small("oinvs", (128, NT))
    assert sm_off[0] <= OFF_S + 10 * KB, sm_off[0]

    pb = [Bank(nc.alloc_psum_tensor(f"pb{i}", [128, 512], F32).ap()) for i in range(6)]
    pt = [Bank(nc.alloc_psum_tensor(f"pt{i}", [128, 1024], BF16, side="right").ap())
          for i in range(2)]

    def dma(eng, out_ap, in_ap, after=(), slow=False):
        return b.d(eng, lambda h, o=out_ap, i=in_ap, sl=slow: [
            h.dma_start(out=o, in_=i, allow_slow_non_contiguous=sl)],
            1, after=after)

    def bcast(eng, dst_sb, src_row, after=()):
        n = dst_sb.shape[-1]
        return dma(eng, dst_sb, src_row.broadcast_to((128, n)), after=after)

    def vop(fn, after=()):
        return b.c("vector", fn, after=after, sem="tV")

    def sop(fn, after=()):
        return b.c("scalar", fn, after=after, sem="tS")

    def top(fn, after=()):
        return b.c("tensor", fn, after=after, sem="tT")

    def gop(fn, after=()):
        return b.c("gpsimd", fn, after=after, sem="cc")

    def ag(cin, cout, after):
        return gop(lambda h: h.collective_compute(
            "AllGather", OP.bypass, replica_groups=[list(range(8))],
            ins=[cin.tensor.ap()], outs=[cout.tensor.ap()]), after=after)

    t_id32 = dma("scalar", id32, id32_in)
    t_cvec = bcast("scalar", cvec, cvec_in[0:1, :])
    t_idbf = dma("scalar", idbf, idbf_in)
    t_rmk = bcast("scalar", rmk, rmask_in[0:1, :])
    t_gateT = dma("scalar", gateTs.rearrange("p (c e) -> p c e", e=8),
                  gateT.rearrange("(c p) e -> p c e", p=128))
    t_bq = dma("scalar", bq_sb, bqkv[0, :].rearrange("(j p) -> p j", p=128),
               slow=True)
    t_c = dma("scalar", silc, c01[0, :].rearrange("(j p) -> p j", p=128), slow=True)
    t_sil = sop(lambda h: h.activation(silc, silc, AF.Silu), after=[t_c])

    wm_ld = {}
    wm_mm = {}
    for L in range(2):
        for k in range(DC):
            prevmm = wm_mm.get((L, k - 2)) or (wm_mm.get((L - 1, DC - 2 + k))
                                               if L and k < 2 else None)
            wm_ld[(L, k)] = dma("sync", wmodb[k % 2], wmod[L][k * 128:(k + 1) * 128, :],
                                after=[prevmm] if prevmm else [])

            def mkmm(L=L, k=k):
                def f(h):
                    last = None
                    for n in range(3):
                        last = h.matmul(pb[3 * L + n].ap[0:1, :],
                                        silc[:, L * DC + k:L * DC + k + 1],
                                        wmodb[k % 2][:, n * 512:(n + 1) * 512],
                                        start=(k == 0), stop=(k == DC - 1))
                    return last
                return f
            wm_mm[(L, k)] = top(mkmm(), after=[wm_ld[(L, k)], t_sil,
                                               wm_mm.get((L, k - 1))])
    mod_ev = []
    for L in range(2):
        for n in range(3):
            bk = pb[3 * L + n]
            bk.wtok = wm_mm[(L, DC - 1)]
            off = L * 1536 + n * 512
            bk.rtok = vop(lambda h, bk=bk, off=off: h.tensor_copy(
                mod_sb[0:1, off:off + 512], bk.ap[0:1, :]), after=[bk.wtok])
            mod_ev.append(bk.rtok)
    t_modst = dma("sync", cc_mod_in[0:1, :], mod_sb[0:1, :], after=mod_ev)
    t_ag_mod = ag(cc_mod_in, cc_mod_out, [t_modst])
    t_mod = [dma("sync", modL[L][0, :].rearrange("(r n) -> r n", n=1536),
                 cc_mod_out[:, L * 1536:(L + 1) * 1536], after=[t_ag_mod])
             for L in range(2)]

    wz = vop(lambda h: h.memset(wab, 0.0))
    red_prev = {0: wz, 1: wz}
    last_red = wz
    wst_last_read = {0: None, 1: None}
    for wi, wname in enumerate("ifgo"):
        for k in range(DC):
            bi = k % 2
            tl = dma("sync", wst[bi], WT[wname][k * 128:(k + 1) * 128, :],
                     after=[wst_last_read[bi]])
            r1 = vop(lambda h, bi=bi: h.reduce_sum(
                stat[:, 0:1], wst[bi], axis=AX, apply_absolute_value=True),
                after=[tl, last_red])
            last_red = vop(lambda h, wi=wi: h.tensor_add(
                wab[:, wi:wi + 1], wab[:, wi:wi + 1], stat[:, 0:1]), after=[r1])
            wst_last_read[bi] = r1
    t_onf = vop(lambda h: h.memset(onf, 1.0))
    pb[0].wtok = top(lambda h: h.matmul(pb[0].ap[0:1, 0:4], onf, wab[:, 0:4],
                                        start=True, stop=True),
                     after=[last_red, t_onf, pb[0].rtok])
    t_mc = vop(lambda h: h.tensor_scalar(sc1[0:1, 0:4], pb[0].ap[0:1, 0:4],
                                         float(1.0 / (D * D)), 1e-5, OP.mult, OP.max),
               after=[pb[0].wtok])
    pb[0].rtok = t_mc
    t_mcr = vop(lambda h: h.reciprocal(sc1[0:1, 4:8], sc1[0:1, 0:4]), after=[t_mc])
    t_swst = dma("sync", smallv[0:1, 0:8], sc1[0:1, 0:8], after=[t_mcr])
    t_swv = dma("scalar", swv, smallv[0:1, 0:8].broadcast_to((128, 8)),
                after=[t_swst])

    if nphase < 1:
        return _finalize(nc, b, taps, y_out)

    tm0_free = [None]
    tm1_free = [None]
    srow_free = [None]
    statg = {0: [], 1: []}

    def ln_tile(x_src, t, bca, bcs, extra=()):
        cb = 32 * (t % 2)
        st = lambda c: statw[:, cb + c:cb + c + 1]
        guard = list(statg[t % 2])
        statg[t % 2] = []
        tx = dma("sync", tm0, x_src[t * 128:(t + 1) * 128, :],
                 after=[tm0_free[0]] + list(extra))
        t1 = vop(lambda h: h.reduce_sum(st(1), tm0, axis=AX), after=[tx] + guard)
        t2 = sop(lambda h: h.activation(tm1, tm0, AF.Square),
                 after=[tx] + tm1_free)
        t3 = vop(lambda h: h.reduce_sum(st(2), tm1, axis=AX), after=[t2])
        t4 = vop(lambda h: h.tensor_scalar_mul(st(1), st(1),
                                               float(1.0 / D)), after=[t1])
        t5 = vop(lambda h: h.tensor_mul(st(3), st(1), st(1)), after=[t4])
        t6 = vop(lambda h: h.tensor_scalar_mul(st(2), st(2),
                                               float(1.0 / D)), after=[t3])
        t7 = vop(lambda h: h.tensor_sub(st(2), st(2), st(3)), after=[t5, t6])
        t8 = sop(lambda h: h.activation(st(4), st(2), AF.Sqrt,
                                        bias=cvec[:, 0:1]), after=[t7, t_cvec])
        t9 = vop(lambda h: h.reciprocal(st(5), st(4)), after=[t8])
        ta = vop(lambda h: h.tensor_scalar(tm1, tm0, st(1), st(5),
                                           OP.subtract, OP.mult), after=[t9, t3])
        tb = vop(lambda h: h.tensor_tensor(tm1, tm1, bca, OP.mult), after=[ta])
        tc = vop(lambda h: h.tensor_tensor(tm1, tm1, bcs, OP.add), after=[tb])
        tm0_free[0] = ta
        statg[t % 2].append(t8)
        return tc

    def quant_tile(src_sb, t, nrm_bc, kq_dst, invs_col, srec_row, after,
                   eps=1e-8, keep_invs=None, region=0):
        cb = 32 * (t % 2) + 6 + 10 * region
        st = lambda c: statw[:, cb + c:cb + c + 1]
        guard = list(statg[t % 2])
        t2 = sop(lambda h: h.activation(tm0, src_sb, AF.Square),
                 after=list(after) + [tm0_free[0]])
        t3 = vop(lambda h: h.reduce_sum(st(0), tm0, axis=AX), after=[t2] + guard)
        epscol = 0 if eps == 1e-5 else 1
        t8 = sop(lambda h, epscol=epscol: h.activation(
            st(1), st(0), AF.Sqrt,
            bias=cvec[:, epscol:epscol + 1], scale=float(1.0 / D)),
            after=[t3, t_cvec])
        t9 = vop(lambda h: h.reciprocal(st(2), st(1)), after=[t8])
        ta = vop(lambda h: h.scalar_tensor_tensor(tm0, src_sb, st(2), nrm_bc,
                                                  OP.mult, OP.mult), after=[t9, t3])
        tb = vop(lambda h: h.reduce_max(st(3), tm0, axis=AX,
                                        apply_absolute_value=True), after=[ta])
        tc = vop(lambda h: h.tensor_scalar_max(st(4), st(3), 1e-5),
                 after=[tb])
        td = vop(lambda h: h.tensor_scalar_mul(st(6), st(4), float(1.0 / 127.0)),
                 after=[tc])
        te = vop(lambda h: h.reciprocal(st(5), st(4)), after=[tc])
        tf = vop(lambda h: h.tensor_scalar_mul(st(5), st(5), 127.0),
                 after=[te])
        tg = vop(lambda h: h.tensor_scalar(tm0, tm0, st(5), MAGIC,
                                           OP.mult, OP.add), after=[tf])
        th = vop(lambda h: h.tensor_scalar(tm0, tm0, MAGIC, 127.0,
                                           OP.subtract, OP.min), after=[tg])
        ti = vop(lambda h: h.tensor_scalar_max(tm0, tm0, -128.0), after=[th])
        toks = []
        for half in range(4):
            bk = pb[4 + (half % 2)]
            def mkT(half=half, bk=bk):
                def f(h):
                    last = None
                    for j in range(4):
                        dc = half * 4 + j
                        last = h.transpose(bk.ap[:, j * 128:(j + 1) * 128],
                                           tm0[:, dc * 128:(dc + 1) * 128], id32)
                    return last
                return f
            bk.wtok = top(mkT(), after=[ti, bk.rtok, t_id32])
            def mkE(half=half, bk=bk):
                def f(h):
                    last = None
                    for j in range(4):
                        dc = half * 4 + j
                        last = h.tensor_copy(
                            kq_dst[:, dc * TOK + t * 128:dc * TOK + (t + 1) * 128],
                            bk.ap[:, j * 128:(j + 1) * 128])
                    return last
                return f
            bk.rtok = vop(mkE(), after=[bk.wtok])
            toks.append(bk.rtok)
        tm0_free[0] = toks[-1]
        statg[t % 2].append(t8)
        if keep_invs is not None:
            tk = vop(lambda h: h.tensor_scalar(
                oinv[:, t:t + 1], st(6),
                swv[:, keep_invs:keep_invs + 1], None, OP.mult),
                after=[td, t_swv])
            toks.append(tk)
        if srec_row is not None:
            bk = pb[3]
            bk.wtok = top(lambda h, bk=bk, st=st: h.transpose(
                bk.ap[0:1, 0:128], st(6), id32),
                after=[td, bk.rtok, t_id32])
            tse = vop(lambda h, bk=bk: h.tensor_copy(srow[0:1, 0:128],
                                                     bk.ap[0:1, 0:128]),
                      after=[bk.wtok, srow_free[0]])
            bk.rtok = tse
            tsd = dma("sync", srec_d[srec_row:srec_row + 1, t * 128:(t + 1) * 128],
                      srow[0:1, 0:128], after=[tse])
            srow_free[0] = tsd
            toks.append(tsd)
            statg[t % 2].append(bk.wtok)
        return toks

    def stream_quant_w(wname, och, buf_i, after):
        wi = "ifgo".index(wname)
        tl = dma("sync", wst[buf_i].rearrange("p (c q) -> p c q", q=128),
                 WT[wname][:, och * 128:(och + 1) * 128].rearrange(
                     "(c p) q -> p c q", p=128), after=after)
        q1 = vop(lambda h, bi=buf_i, wi=wi: h.tensor_scalar(
            wst[bi], wst[bi], swv[:, 4 + wi:5 + wi], MAGIC, OP.mult, OP.add),
            after=[tl, t_swv])
        q2 = vop(lambda h, bi=buf_i: h.tensor_scalar(
            wst[bi], wst[bi], MAGIC, 1.0, OP.subtract, OP.min), after=[q1])
        q3 = vop(lambda h, bi=buf_i: h.tensor_scalar_max(mq[bi], wst[bi], -1.0),
                 after=[q2])
        return q3

    def bit_mm(kq_src, mqb, bank, n, after):
        def f(h):
            last = None
            for dc in range(DC):
                last = h.matmul(bank.ap,
                                mqb[:, dc * 128:(dc + 1) * 128],
                                kq_src[:, dc * TOK + n * 512:dc * TOK + (n + 1) * 512],
                                start=(dc == 0), stop=(dc == DC - 1))
            return last
        return top(f, after=list(after) + [bank.rtok])

    t_bca = bcast("scalar", bc[0], modL[0][0:1, D:2 * D], after=[t_mod[0]])
    t_bca = vop(lambda h: h.tensor_scalar_add(bc[0], bc[0], 1.0), after=[t_bca])
    t_bcs = bcast("scalar", bc[1], modL[0][0:1, 0:D], after=[t_mod[0]])
    t_bni = bcast("scalar", bc[2], nrm["i"][0:1, :])
    t_bnf = bcast("scalar", bc[3], nrm["f"][0:1, :])

    sw1_done = []
    for t in range(NT):
        th = ln_tile(x_in, t, bc[0], bc[1], extra=[t_bca, t_bcs, t_bni, t_bnf])
        hst = dma("sync", scr_h[t * 128:(t + 1) * 128, :], tm1, after=[th])
        ks_i = quant_tile(tm1, t, bc[2], kq_i, 12, 0, after=[th])
        ks_f = quant_tile(tm1, t, bc[3], kq_f, 13, 1, after=[th], region=1)
        tm1_free[:] = [ks_f[0], hst]
        sw1_done += ks_i + ks_f + [hst]

    t_sri = bcast("scalar", sreb[0], srec_d[0:1, :], after=sw1_done)
    t_sri = vop(lambda h: h.tensor_scalar_mul(sreb[0], sreb[0], swv[:, 0:1]),
                after=[t_sri, t_swv])
    t_srf = bcast("scalar", sreb[1], srec_d[1:2, :], after=sw1_done)
    t_srf = vop(lambda h: h.tensor_scalar_mul(sreb[1], sreb[1], swv[:, 1:2]),
                after=[t_srf, t_swv])

    lz = vop(lambda h: h.memset(lnf, 0.0))
    scr_store_toks = []
    prev = {"q": [wst_last_read[0], wst_last_read[1]], "scanio_f": None,
            "scanio_i": None, "ev0": None, "ev1": None, "ev2": None, "scano": None}
    for och in range(16):
        qi = stream_quant_w("i", och, 0, after=[prev["q"][0]])
        qf = stream_quant_w("f", och, 1, after=[prev["q"][1]])
        mm = {}
        for wkey, kqs, mqi, bks in (("i", kq_i, 0, (0, 1)), ("f", kq_f, 1, (2, 3))):
            for n in range(2):
                bk = pb[bks[n]]
                bk.wtok = bit_mm(kqs, mq[mqi], bk, n,
                                 after=[qi if wkey == "i" else qf] + sw1_done)
                mm[(wkey, n)] = bk.wtok
        prev["q"] = [mm[("i", 1)], mm[("f", 1)]]
        ef = []
        for n in range(2):
            bk = pb[2 + n]
            e1 = vop(lambda h, bk=bk, n=n: h.tensor_tensor(
                ev[0][:, n * 512:(n + 1) * 512], bk.ap,
                sreb[1][:, n * 512:(n + 1) * 512], OP.mult),
                after=[bk.wtok, t_srf, prev["ev0"]])
            bk.rtok = e1
            ef.append(e1)
        e3 = sop(lambda h: h.activation(scanio[:, 0:TOK], ev[0], AF.Sigmoid),
                 after=ef + [prev["scanio_f"]])
        prev["ev0"] = e3
        ei = []
        for n in range(2):
            bk = pb[0 + n]
            e1 = vop(lambda h, bk=bk, n=n: h.tensor_tensor(
                ev[1][:, n * 512:(n + 1) * 512], bk.ap,
                sreb[0][:, n * 512:(n + 1) * 512], OP.mult),
                after=[bk.wtok, t_sri, prev["ev1"]])
            bk.rtok = e1
            ei.append(e1)
        e5 = sop(lambda h: h.activation(ev[1], ev[1], AF.Silu), after=ei)
        e6 = vop(lambda h: h.tensor_scalar(ev[2], scanio[:, 0:TOK], -1.0, -1.0,
                                           OP.mult, OP.subtract),
                 after=[e3, prev["ev2"]])
        e7 = vop(lambda h: h.tensor_tensor(scanio[:, TOK:2 * TOK], ev[1], ev[2],
                                           OP.mult),
                 after=[e5, e6, prev["scanio_i"]])
        prev["ev1"] = e7
        prev["ev2"] = e7
        s1 = vop(lambda h: h.tensor_tensor_scan(
            scano, scanio[:, 0:TOK], scanio[:, TOK:2 * TOK], 0.0, OP.mult, OP.add),
            after=[e7, prev["scano"]])
        s2 = vop(lambda h, och=och: h.tensor_copy(hcar[:, och:och + 1],
                                                  scano[:, TOK - 1:TOK]), after=[s1])
        prev["scano"] = s2
        f1 = sop(lambda h: h.activation(ev[2], scanio[:, 0:TOK], AF.Ln),
                 after=[e7, e6])
        f2 = vop(lambda h: h.reduce_sum(stat[:, 0:1], ev[2], axis=AX), after=[f1])
        f3 = vop(lambda h, och=och: h.tensor_add(lnf[:, och:och + 1],
                                                 lnf[:, och:och + 1], stat[:, 0:1]),
                 after=[f2, lz])
        prev["ev2"] = f1
        d1 = dma("sync", scr_f[och * 128:(och + 1) * 128, :], scanio[:, 0:TOK],
                 after=[e3])
        d2 = dma("sync", scr_i[och * 128:(och + 1) * 128, :], scanio[:, TOK:2 * TOK],
                 after=[e7])
        prev["scanio_f"] = d1
        prev["scanio_i"] = d2
        prev["ev0"] = e3
        scr_store_toks += [d1, d2]
        sw1_last = [d1, d2, s2, f3]

    t_fexp = sop(lambda h: h.activation(facc, lnf, AF.Exp), after=[sw1_last[3]])
    t_hg1 = dma("sync", cc_hg_in[0, 0:D].rearrange("(j p) -> p j", p=128), facc,
                after=[t_fexp], slow=True)
    t_hg2 = dma("sync", cc_hg_in[0, D:2 * D].rearrange("(j p) -> p j", p=128), hcar,
                after=[sw1_last[2]], slow=True)
    t_ag_hg = ag(cc_hg_in, cc_hg_out, [t_hg1, t_hg2])
    t_cz = vop(lambda h: h.memset(carry, 0.0))
    comb = t_cz
    for j in range(7):
        gF = dma("scalar", ghF, cc_hg_out[j, 0:D].rearrange("(q p) -> p q", p=128),
                 after=[t_ag_hg, comb], slow=True)
        gH = dma("scalar", ghH, cc_hg_out[j, D:2 * D].rearrange("(q p) -> p q", p=128),
                 after=[t_ag_hg, comb], slow=True)
        c1_ = vop(lambda h: h.tensor_mul(gtmp, ghF, carry), after=[gF, comb])
        c2_ = vop(lambda h: h.tensor_add(gtmp, gtmp, ghH), after=[gH, c1_])
        c3_ = vop(lambda h: h.tensor_sub(gtmp, gtmp, carry), after=[c2_])
        c4_ = vop(lambda h, j=j: h.tensor_scalar_mul(gtmp, gtmp, rmk[:, j:j + 1]),
                  after=[c3_, t_rmk])
        comb = vop(lambda h: h.tensor_add(carry, carry, gtmp), after=[c4_])
    t_carry = comb

    if nphase < 2:
        return _finalize(nc, b, taps, y_out)

    t_bng = bcast("scalar", bc[2], nrm["g"][0:1, :], after=sw1_done)
    sw2_done = []
    for t in range(NT):
        tl = dma("sync", tm1, scr_h[t * 128:(t + 1) * 128, :],
                 after=tm1_free + sw1_done)
        ks = quant_tile(tm1, t, bc[2], kq_g, 12, 2, after=[tl, t_bng])
        tm1_free[:] = [ks[0]]
        sw2_done += ks
    t_srg = bcast("scalar", sreb_g, srec_d[2:3, :], after=sw2_done)
    t_srg = vop(lambda h: h.tensor_scalar_mul(sreb_g, sreb_g, swv[:, 2:3]),
                after=[t_srg, t_swv])

    prevq = prev["q"][0]
    g_ev0_free = None
    g_est = None
    g_est_all = []
    for och in range(16):
        qg = stream_quant_w("g", och, och % 2, after=[prevq])
        mmg = []
        for n in range(2):
            bk = pb[n]
            bk.wtok = bit_mm(kq_g, mq[och % 2], bk, n, after=[qg] + sw2_done)
            mmg.append(bk.wtok)
        prevq = mmg[1]
        eg = []
        for n in range(2):
            bk = pb[n]
            e1 = vop(lambda h, bk=bk, n=n: h.tensor_tensor(
                ev2[0][:, n * 512:(n + 1) * 512], bk.ap,
                sreb_g[:, n * 512:(n + 1) * 512], OP.mult),
                after=[bk.wtok, t_srg, g_ev0_free])
            bk.rtok = e1
            eg.append(e1)
        evtoks = []
        for half in range(2):
            bk = pb[2 + half]
            def mkT(half=half, bk=bk):
                def f(h):
                    last = None
                    for j in range(4):
                        tt = half * 4 + j
                        last = h.transpose(bk.ap[:, j * 128:(j + 1) * 128],
                                           ev2[0][:, tt * 128:(tt + 1) * 128], id32)
                    return last
                return f
            bk.wtok = top(mkT(), after=eg + [bk.rtok, t_id32])
            def mkE(half=half, bk=bk):
                def f(h):
                    last = None
                    for j in range(4):
                        last = h.tensor_copy(ev2[1][:, (half * 4 + j) * 128:
                                                    (half * 4 + j + 1) * 128],
                                             bk.ap[:, j * 128:(j + 1) * 128])
                    return last
                return f
            bk.rtok = vop(mkE(), after=[bk.wtok, g_est])
            evtoks.append(bk.rtok)
        g_ev0_free = pb[3].wtok
        g_est = b.d("sync",
                    lambda h, och=och: [h.dma_start(
                        out=scr_g[tt * 128:(tt + 1) * 128, och * 128:(och + 1) * 128],
                        in_=ev2[1][:, tt * 128:(tt + 1) * 128]) for tt in range(8)],
                    8, after=evtoks)
        g_est_all.append(g_est)

    if nphase < 3:
        return _finalize(nc, b, taps, y_out)

    t_bgn = bcast("scalar", bc[3], gnw_in[0:1, :], after=sw2_done)
    t_bno = bcast("scalar", bc[1], nrm["o"][0:1, :], after=sw1_done)
    carryB = carry
    pb_done = []
    prev_sc = [t_carry]
    htile_free = [None]
    gtile_free = [None]
    for t in range(NT):
        httoks = []
        for och in range(16):
            lf = dma("sync", scn2[:, 0:128],
                     scr_f[och * 128:(och + 1) * 128, t * 128:(t + 1) * 128],
                     after=list(prev_sc) + scr_store_toks)
            li = dma("sync", scn2[:, 128:256],
                     scr_i[och * 128:(och + 1) * 128, t * 128:(t + 1) * 128],
                     after=list(prev_sc) + scr_store_toks)
            s1 = vop(lambda h, och=och: h.tensor_tensor_scan(
                scn2o, scn2[:, 0:128], scn2[:, 128:256],
                carryB[:, och:och + 1], OP.mult, OP.add),
                after=[lf, li, t_carry])
            s2 = vop(lambda h, och=och: h.tensor_copy(
                carryB[:, och:och + 1], scn2o[:, 127:128]), after=[s1])
            bk = pb[4 + (och % 2)]
            bk.wtok = top(lambda h, bk=bk: h.transpose(
                bk.ap[:, 0:128], scn2o, id32), after=[s1, bk.rtok, t_id32])
            bk.rtok = vop(lambda h, bk=bk, och=och: h.tensor_copy(
                htile[:, och * 128:(och + 1) * 128], bk.ap[:, 0:128]),
                after=[bk.wtok, htile_free[0] if och == 0 else None])
            httoks.append(bk.rtok)
            prev_sc = [s2, bk.wtok]
        lg = dma("sync", gtile, scr_g[t * 128:(t + 1) * 128, :],
                 after=g_est_all + [gtile_free[0]])
        r2 = sop(lambda h: h.activation(tm0, gtile, AF.Square),
                 after=[lg, tm0_free[0]])
        cbr = 32 * (t % 2) + 26
        rst = lambda c: statw[:, cbr + c:cbr + c + 1]
        rguard = list(statg[t % 2])
        r3 = vop(lambda h: h.reduce_sum(rst(0), tm0, axis=AX),
                 after=[r2] + rguard)
        r8 = sop(lambda h: h.activation(rst(1), rst(0), AF.Sqrt,
                                        bias=cvec[:, 0:1], scale=float(1.0 / D)),
                 after=[r3, t_cvec])
        r9 = vop(lambda h: h.reciprocal(rst(2), rst(1)), after=[r8])
        ra = vop(lambda h: h.scalar_tensor_tensor(gtile, gtile, rst(2), bc[3],
                                                  OP.mult, OP.mult),
                 after=[r9, t_bgn, r2])
        statg[t % 2].append(r8)
        tdb1 = None
        if "dbg1" in taps:
            tdb1 = dma("sync", dbg1[t * 128:(t + 1) * 128, :], htile, after=httoks)
        rb = sop(lambda h: h.activation(tm1, htile, AF.Silu),
                 after=httoks + tm1_free)
        htile_free[0] = tdb1 or rb
        rc = vop(lambda h: h.tensor_tensor(tm1, tm1, gtile, OP.mult), after=[rb, ra])
        gtile_free[0] = rc
        tdb2 = None
        if "dbg2" in taps:
            tdb2 = dma("sync", dbg2[t * 128:(t + 1) * 128, :], tm1, after=[rc])
        ks = quant_tile(tm1, t, bc[1], kq_o, 12, None, after=[rc, t_bno],
                        keep_invs=3)
        tm1_free[:] = [ks[0]] + ([tdb2] if tdb2 else [])
        pb_done += ks
    sw3_done = pb_done

    if "dbg3" in taps:
        dma("sync", dbg3, kq_o, after=sw3_done)

    t_bga = bcast("scalar", bc[0], modL[0][0:1, 2 * D:3 * D],
                  after=[t_mod[0]] + sw1_done)
    t_m = vop(lambda h: h.tensor_scalar(bc[2], bc[0], 0.0, None, OP.is_ge),
              after=[t_bga] + sw2_done)
    t_a = vop(lambda h: h.tensor_scalar_max(bc[3], bc[0], 0.25),
              after=[t_bga, gtile_free[0], t_m])
    t_b = vop(lambda h: h.tensor_scalar_min(bc[0], bc[0], -0.25), after=[t_a])
    t_c1 = vop(lambda h: h.tensor_sub(bc[3], bc[3], bc[0]), after=[t_b])
    t_c2 = vop(lambda h: h.tensor_mul(bc[3], bc[3], bc[2]), after=[t_c1])
    t_ga = vop(lambda h: h.tensor_add(bc[0], bc[0], bc[3]), after=[t_c2])
    if "dbg4" in taps:
        tdga = dma("sync", dbg4[:, 0:D], bc[0], after=[t_ga])
        dma("sync", dbg4[:, 0:NT], oinv, after=sw3_done + [tdga])
    if "dbg5" in taps:
        dma("sync", dbg5[0:1, 0:8], sc1[0:1, 0:8], after=[t_swst])

    import os as _os
    if _os.environ.get("WO_SERIAL") == "1":
        b.serial = True
    prev_x1 = [None] * 2
    x1_store_toks = []
    prev_dump = None
    for och in range(16):
        qo = stream_quant_w("o", och, och % 2, after=[prevq])
        mmo = []
        for n in range(2):
            bk = pb[n]
            bk.wtok = bit_mm(kq_o, mq[och % 2], bk, n, after=[qo] + sw3_done)
            mmo.append(bk.wtok)
        prevq = mmo[1]
        eo = []
        for n in range(2):
            bk = pb[n]
            e1 = vop(lambda h, bk=bk, n=n: h.tensor_copy(
                ev2[0][:, n * 512:(n + 1) * 512], bk.ap), after=[bk.wtok, prev_dump])
            bk.rtok = e1
            eo.append(e1)
        if "dbg6" in taps:
            prev_dump = dma("sync", dbg6[och * 128:(och + 1) * 128, :], ev2[0],
                            after=eo)
        for half in range(2):
            bk = pb[2 + half]
            def mkT(half=half, bk=bk):
                def f(h):
                    last = None
                    for j in range(4):
                        tt = half * 4 + j
                        last = h.transpose(bk.ap[:, j * 128:(j + 1) * 128],
                                           ev2[0][:, tt * 128:(tt + 1) * 128], id32)
                    return last
                return f
            bk.wtok = top(mkT(), after=eo + [bk.rtok, t_id32])
            exs = []
            for j in range(4):
                tt = half * 4 + j
                lx = dma("scalar", tm0[:, (half * 4 + j) * 128:
                                       (half * 4 + j + 1) * 128],
                         x_in[tt * 128:(tt + 1) * 128, och * 128:(och + 1) * 128],
                         after=[tm0_free[0], prev_x1[half]])
                exs.append(lx)

            def mkE(half=half, bk=bk, och=och):
                def f(h):
                    last = None
                    for j in range(4):
                        tt = half * 4 + j
                        sl = slice(tt * 128, (tt + 1) * 128)
                        last = h.scalar_tensor_tensor(
                            ev2[1][:, sl], bk.ap[:, j * 128:(j + 1) * 128],
                            oinv[:, tt:tt + 1],
                            bc[0][:, och * 128:(och + 1) * 128],
                            OP.mult, OP.mult)
                    return last
                return f
            tev = vop(mkE(), after=[bk.wtok, t_ga, prev_x1[half]] + g_est_all)
            def mkE2(half=half):
                def f(h):
                    last = None
                    for j in range(4):
                        tt = half * 4 + j
                        sl = slice(tt * 128, (tt + 1) * 128)
                        last = h.tensor_add(ev2[1][:, sl], ev2[1][:, sl],
                                            tm0[:, sl])
                    return last
                return f
            tev2 = vop(mkE2(), after=[tev] + exs)
            bk.rtok = tev2
            tm0_free[0] = tev2
            if "dbg7" in taps:
                tev2 = dma("sync", dbg7[och * 128:(och + 1) * 128,
                                        half * 512:(half + 1) * 512],
                           ev2[1][:, half * 512:(half + 1) * 512], after=[tev2])
            st = b.d("sync",
                     lambda h, half=half, och=och: [h.dma_start(
                         out=xres[(half * 4 + j) * 128:(half * 4 + j + 1) * 128,
                                  och * 128:(och + 1) * 128],
                         in_=ev2[1][:, (half * 4 + j) * 128:(half * 4 + j + 1) * 128])
                         for j in range(4)],
                     4, after=[tev2])
            prev_x1[half] = st
            x1_store_toks.append(st)
    t_x1 = x1_store_toks

    if nphase < 4:
        return _finalize(nc, b, taps, y_out)

    b.serial = False
    l0_done = x1_store_toks + sw3_done + [t_ga]

    q8t = at("q8t_s", OFF_P + 0, (128, D), FP8)
    q8b = at("q8b_s", OFF_P + 2 * KB, (128, D), BF16)
    insb = small("insbs", (1, 1), BF16)
    t_lsi = dma("scalar", insb, ins_in[0:1, 0:1])
    t_lw8 = dma("scalar", sc1[0:1, 18:19], w8s_in[0:1, 0:1])
    t_s32 = vop(lambda h: h.tensor_copy(sc1[0:1, 16:17], insb), after=[t_lsi])
    t_rs = vop(lambda h: h.reciprocal(sc1[0:1, 17:18], sc1[0:1, 16:17]),
               after=[t_s32])
    t_c = vop(lambda h: h.tensor_mul(sc1[0:1, 18:19], sc1[0:1, 18:19],
                                     sc1[0:1, 16:17]), after=[t_s32, t_lw8])
    t_scst = dma("sync", smallv[0:1, 16:19], sc1[0:1, 16:19], after=[t_rs, t_c])
    t_scbc = dma("scalar", c128[:, 0:3],
                 smallv[0:1, 16:19].broadcast_to((128, 3)), after=[t_scst])
    t_fb1 = bcast("scalar", bc[1], modL[0][0:1, 4 * D:5 * D],
                  after=[t_mod[0]] + l0_done)
    t_fb1 = vop(lambda h: h.tensor_scalar_add(bc[1], bc[1], 1.0), after=[t_fb1])
    t_fb2 = bcast("scalar", bc[2], modL[0][0:1, 3 * D:4 * D],
                  after=[t_mod[0]] + l0_done)
    t_fb3 = bcast("scalar", bc[3], modL[0][0:1, 5 * D:6 * D],
                  after=[t_mod[0]] + l0_done)
    t_gm = vop(lambda h: h.tensor_scalar(bc[0], bc[3], 0.0, None, OP.is_ge),
               after=[t_fb3] + l0_done)
    t_ga2_ = vop(lambda h: h.tensor_scalar_max(wst[0], bc[3], 0.25),
                 after=[t_fb3, t_gm])
    t_gb2_ = vop(lambda h: h.tensor_scalar_min(bc[3], bc[3], -0.25), after=[t_ga2_])
    t_gc1 = vop(lambda h: h.tensor_sub(wst[0], wst[0], bc[3]), after=[t_gb2_])
    t_gc2 = vop(lambda h: h.tensor_mul(wst[0], wst[0], bc[0]), after=[t_gc1])
    t_gb = vop(lambda h: h.tensor_add(bc[3], bc[3], wst[0]), after=[t_gc2])
    t_gbe = vop(lambda h: h.tensor_scalar_mul(bc[3], bc[3], c128[:, 2:3]),
                after=[t_gb, t_scbc])

    ffn_q_done = []
    for t in range(NT):
        tl = dma("sync", tm0, xres[t * 128:(t + 1) * 128, :],
                 after=[tm0_free[0]] + x1_store_toks)
        tm0_free[0] = tl
        th = None
        cb2 = 32 * (t % 2)
        st_ = lambda c, cb2=cb2: statw[:, cb2 + c:cb2 + c + 1]
        guard = list(statg[t % 2])
        statg[t % 2] = []
        t1 = vop(lambda h, st_=st_: h.reduce_sum(st_(1), tm0, axis=AX),
                 after=[tl] + guard)
        t2 = sop(lambda h: h.activation(tm1, tm0, AF.Square), after=[tl] + tm1_free)
        t3 = vop(lambda h, st_=st_: h.reduce_sum(st_(2), tm1, axis=AX), after=[t2])
        t4 = vop(lambda h, st_=st_: h.tensor_scalar_mul(st_(1), st_(1),
                                                        float(1.0 / D)), after=[t1])
        t5 = vop(lambda h, st_=st_: h.tensor_mul(st_(3), st_(1), st_(1)), after=[t4])
        t6 = vop(lambda h, st_=st_: h.tensor_scalar_mul(st_(2), st_(2),
                                                        float(1.0 / D)), after=[t3])
        t7 = vop(lambda h, st_=st_: h.tensor_sub(st_(2), st_(2), st_(3)),
                 after=[t5, t6])
        t8 = sop(lambda h, st_=st_: h.activation(st_(4), st_(2), AF.Sqrt,
                                                 bias=cvec[:, 0:1]),
                 after=[t7, t_cvec])
        t9 = vop(lambda h, st_=st_: h.reciprocal(st_(5), st_(4)), after=[t8])
        ta = vop(lambda h, st_=st_: h.tensor_scalar(tm1, tm0, st_(1), st_(5),
                                                    OP.subtract, OP.mult),
                 after=[t9, t3])
        tb = vop(lambda h: h.tensor_tensor(tm1, tm1, bc[1], OP.mult),
                 after=[ta, t_fb1])
        tc = vop(lambda h: h.tensor_tensor(tm1, tm1, bc[2], OP.add),
                 after=[tb, t_fb2])
        statg[t % 2].append(t8)
        tm0_free[0] = ta
        q1 = vop(lambda h: h.tensor_scalar(tm1, tm1, c128[:, 1:2], 448.0,
                                           OP.mult, OP.min), after=[tc, t_scbc])
        q2 = vop(lambda h: h.tensor_scalar_max(tm1, tm1, -448.0), after=[q1])
        q3 = sop(lambda h: h.copy(q8t, tm1), after=[q2] + ffn_q_done[-1:])
        q4 = sop(lambda h: h.copy(q8b, q8t), after=[q3])
        tm1_free[:] = [q3]
        for half in range(2):
            bkp = pt[half]
            def mkT(half=half, bkp=bkp):
                def f(h):
                    last = None
                    for j in range(8):
                        dc = half * 8 + j
                        last = h.transpose(bkp.ap[:, j * 128:(j + 1) * 128],
                                           q8b[:, dc * 128:(dc + 1) * 128], idbf)
                    return last
                return f
            bkp.wtok = top(mkT(), after=[q4, bkp.rtok, t_idbf])
            def mkE(half=half, bkp=bkp, t=t):
                def f(h):
                    last = None
                    for j in range(8):
                        dc = half * 8 + j
                        last = h.tensor_copy(
                            q8T[:, dc * TOK + t * 128:dc * TOK + (t + 1) * 128],
                            bkp.ap[:, j * 128:(j + 1) * 128])
                    return last
                return f
            bkp.rtok = vop(mkE(), after=[bkp.wtok])
            ffn_q_done.append(bkp.rtok)

    x2_store_toks = []
    prev_w8 = [prevq, prevq]
    prev_x2 = {}
    prev_ev20 = [x1_store_toks[-1]]
    for n in range(4):
        tws = []
        for hh in range(2):
            tw = b.d("gpsimd", lambda h, n=n, hh=hh: [h.dma_start(
                out=w8big[hh].rearrange("p (c q) -> p c q", q=512),
                in_=W8T[hh * 1024:(hh + 1) * 1024,
                        n * 512:(n + 1) * 512].rearrange(
                    "(c p) q -> p c q", p=128))],
                1, after=[prev_w8[hh]])
            tws.append(tw)
        for tg in range(2):
            mm_group = []
            for dc in range(DC):
                hh = dc // 8
                def mkM(dc=dc, tg=tg, hh=hh):
                    def f(h):
                        last = None
                        for j in range(4):
                            t = tg * 4 + j
                            last = h.matmul(
                                pb[j].ap,
                                q8T[:, dc * TOK + t * 128:dc * TOK + (t + 1) * 128],
                                w8big[hh][:, (dc % 8) * 512:(dc % 8 + 1) * 512],
                                start=(dc == 0), stop=(dc == DC - 1))
                        return last
                    return f
                mmt = top(mkM(), after=[tws[hh]] + ffn_q_done +
                          [pb[j].rtok for j in range(4)])
                mm_group.append(mmt)
            prev_w8 = [mm_group[-1], mm_group[-1]]
            for j in range(4):
                pb[j].wtok = mm_group[-1]
            for j in range(4):
                t = tg * 4 + j
                bkj = pb[j]
                lx = dma("scalar", tm0[:, j * 512:(j + 1) * 512],
                         xres[t * 128:(t + 1) * 128, n * 512:(n + 1) * 512],
                         after=[tm0_free[0], prev_x2.get((t, n), None)] +
                               x1_store_toks)
                e1 = vop(lambda h, bkj=bkj, n=n: h.tensor_tensor(
                    ev2[0][:, 0:512],
                    bkj.ap, bc[3][:, n * 512:(n + 1) * 512],
                    OP.mult), after=[bkj.wtok, t_gbe, prev_ev20[0]])
                e2 = vop(lambda h, j=j: h.tensor_add(
                    ev2[0][:, 0:512], ev2[0][:, 0:512],
                    tm0[:, j * 512:(j + 1) * 512]),
                    after=[e1, lx])
                bkj.rtok = e2
                tm0_free[0] = e2
                st2 = dma("sync", xres[t * 128:(t + 1) * 128, n * 512:(n + 1) * 512],
                          ev2[0][:, 0:512], after=[e2])
                x2_store_toks.append(st2)
                prev_x2[(t, n)] = st2
                prev_ev20[0] = st2

    if nphase < 5:
        return _finalize(nc, b, taps, y_out)

    attn_gate = x2_store_toks + [t_gbe]
    kvstrip = at("kvstrip", OFF_P + 0, (128, TOK), BF16)
    sqstrip = at("sqstrip", OFF_P + 2 * KB, (128, TOK), BF16)
    vtstrip = at("vtstrip", OFF_P + 4 * KB, (128, TOK), BF16)
    onb = small("onbs", (128, 1), BF16)
    lacc = small("laccs", (128, NT))
    t_onb = vop(lambda h: h.memset(onb, 1.0))

    t_ab1 = bcast("scalar", bc[1], modL[1][0:1, D:2 * D], after=[t_mod[1]] + attn_gate)
    t_ab1 = vop(lambda h: h.tensor_scalar_add(bc[1], bc[1], 1.0), after=[t_ab1])
    t_ab2 = bcast("scalar", bc[2], modL[1][0:1, 0:D], after=[t_mod[1]] + attn_gate)

    hfm_done = []
    for t in range(NT):
        tl = dma("sync", tm0, xres[t * 128:(t + 1) * 128, :],
                 after=[tm0_free[0]] + x2_store_toks)
        tm0_free[0] = tl
        cb2 = 32 * (t % 2)
        st_ = lambda c, cb2=cb2: statw[:, cb2 + c:cb2 + c + 1]
        guard = list(statg[t % 2])
        statg[t % 2] = []
        t1 = vop(lambda h, st_=st_: h.reduce_sum(st_(1), tm0, axis=AX),
                 after=[tl] + guard)
        t2 = sop(lambda h: h.activation(tm1, tm0, AF.Square), after=[tl] + tm1_free)
        t3 = vop(lambda h, st_=st_: h.reduce_sum(st_(2), tm1, axis=AX), after=[t2])
        t4 = vop(lambda h, st_=st_: h.tensor_scalar_mul(st_(1), st_(1),
                                                        float(1.0 / D)), after=[t1])
        t5 = vop(lambda h, st_=st_: h.tensor_mul(st_(3), st_(1), st_(1)), after=[t4])
        t6 = vop(lambda h, st_=st_: h.tensor_scalar_mul(st_(2), st_(2),
                                                        float(1.0 / D)), after=[t3])
        t7 = vop(lambda h, st_=st_: h.tensor_sub(st_(2), st_(2), st_(3)),
                 after=[t5, t6])
        t8 = sop(lambda h, st_=st_: h.activation(st_(4), st_(2), AF.Sqrt,
                                                 bias=cvec[:, 0:1]),
                 after=[t7, t_cvec])
        t9 = vop(lambda h, st_=st_: h.reciprocal(st_(5), st_(4)), after=[t8])
        ta = vop(lambda h, st_=st_: h.tensor_scalar(tm1, tm0, st_(1), st_(5),
                                                    OP.subtract, OP.mult),
                 after=[t9, t3])
        tb = vop(lambda h: h.tensor_tensor(tm1, tm1, bc[1], OP.mult),
                 after=[ta, t_ab1])
        tc = vop(lambda h: h.tensor_tensor(tm1, tm1, bc[2], OP.add),
                 after=[tb, t_ab2])
        statg[t % 2].append(t8)
        tm0_free[0] = ta
        for half in range(2):
            bk = pb[4 + half]
            for qtr in range(2):
                def mkT2(half=half, bk=bk, qtr=qtr):
                    def f(h):
                        last = None
                        for j in range(4):
                            dc = half * 8 + qtr * 4 + j
                            last = h.transpose(bk.ap[:, j * 128:(j + 1) * 128],
                                               tm1[:, dc * 128:(dc + 1) * 128], id32)
                        return last
                    return f
                bk.wtok = top(mkT2(), after=[tc, bk.rtok, t_id32])
                def mkE(half=half, bk=bk, qtr=qtr, t=t):
                    def f(h):
                        last = None
                        for j in range(4):
                            dc = half * 8 + qtr * 4 + j
                            last = h.tensor_copy(
                                hfm[:, dc * TOK + t * 128:dc * TOK + (t + 1) * 128],
                                bk.ap[:, j * 128:(j + 1) * 128])
                        return last
                    return f
                bk.rtok = vop(mkE(), after=[bk.wtok] + (attn_gate if t == 0 else []))
                hfm_done.append(bk.rtok)
        tm1_free[:] = [hfm_done[-1]]

    prev_wq = [x2_store_toks[-1], x2_store_toks[-1]]
    k_stage = []
    v_stage = []
    q_done = []
    kn_mm = None
    qn_mm = None
    strip_free = [x2_store_toks[-1]]
    vt_free = [x2_store_toks[-1]]
    ksq_sb = small("ksq_sbs", (1, TOK))
    qn_sb2 = small("qn2s", (1, TOK))
    ksqb = small("ksqbs", (1, 32), BF16)
    mk_bf = small("mk_bfs", (1, 8), BF16)

    def qkv_block(blk, after_mm=()):
        bi = blk % 2
        tw = b.d("gpsimd",
                 lambda h, blk=blk, bi=bi: [h.dma_start(
                     out=mq[bi].rearrange("p (c q) -> p c q", q=128),
                     in_=WqkvT[:, blk * 128:(blk + 1) * 128].rearrange(
                         "(c p) q -> p c q", p=128))],
                 1, after=[prev_wq[bi]])
        mmq = []
        for n in range(2):
            bk = pb[n]
            def mkM(bi=bi, n=n):
                def f(h):
                    last = None
                    for dc in range(DC):
                        last = h.matmul(
                            pb[n].ap, mq[bi][:, dc * 128:(dc + 1) * 128],
                            hfm[:, dc * TOK + n * 512:dc * TOK + (n + 1) * 512],
                            start=(dc == 0), stop=(dc == DC - 1))
                    return last
                return f
            bk.wtok = top(mkM(), after=[tw, bk.rtok] + hfm_done + list(after_mm))
            mmq.append(bk.wtok)
        prev_wq[bi] = mmq[1]
        return mmq

    for kb in range(16):
        blk = 16 + kb
        qkv_block(blk)
        ev_t = []
        for n in range(2):
            bk = pb[n]
            e1 = vop(lambda h, bk=bk, n=n, blk=blk: h.tensor_scalar_add(
                kvstrip[:, n * 512:(n + 1) * 512],
                bk.ap, bq_sb[:, blk:blk + 1]),
                after=[bk.wtok, t_bq, strip_free[0]] +
                      (x2_store_toks if kb == 0 else []))
            bk.rtok = e1
            ev_t.append(e1)
        sq = sop(lambda h: h.activation(sqstrip, kvstrip, AF.Square),
                 after=ev_t + ([kn_mm] if kn_mm else []))
        def mkN(kb=kb):
            def f(h):
                h.matmul(pb[4].ap[0:1, :], onb, sqstrip[:, 0:512],
                         start=(kb == 0), stop=(kb == 15))
                return h.matmul(pb[5].ap[0:1, :], onb, sqstrip[:, 512:1024],
                                start=(kb == 0), stop=(kb == 15))
            return f
        kn_mm = top(mkN(), after=[sq, t_onb, pb[4].rtok, pb[5].rtok])
        kst = dma("sync",
                  cc_k_in[0, kb * 128 * TOK:(kb + 1) * 128 * TOK].rearrange(
                      "(p t) -> p t", t=TOK),
                  kvstrip, after=ev_t)
        k_stage.append(kst)
        strip_free[0] = kst
    e_k1 = vop(lambda h: h.tensor_copy(ksq_sb[0:1, 0:512], pb[4].ap[0:1, :]),
               after=[kn_mm])
    e_k2 = vop(lambda h: h.tensor_copy(ksq_sb[0:1, 512:1024], pb[5].ap[0:1, :]),
               after=[kn_mm])
    pb[4].rtok = e_k1
    pb[5].rtok = e_k2
    e_k3 = vop(lambda h: h.reduce_max(sc1[0:1, 21:22], ksq_sb, axis=AX),
               after=[e_k1, e_k2])
    e_k0 = vop(lambda h: h.memset(ksqb, 0.0))
    e_k4 = vop(lambda h: h.tensor_copy(ksqb[0:1, 0:1], sc1[0:1, 21:22]),
               after=[e_k3, e_k0])
    kst2 = dma("sync", cc_k_in[0, D * TOK:D * TOK + 32].rearrange(
        "(o n) -> o n", o=1), ksqb, after=[e_k4])
    t_ag_k = ag(cc_k_in, cc_k_out, k_stage + [kst2])

    for vb in range(16):
        blk = 32 + vb
        qkv_block(blk)
        ev_t = []
        for n in range(2):
            bk = pb[n]
            e1 = vop(lambda h, bk=bk, n=n, blk=blk: h.tensor_scalar_add(
                kvstrip[:, n * 512:(n + 1) * 512],
                bk.ap, bq_sb[:, blk:blk + 1]),
                after=[bk.wtok, t_bq, strip_free[0]])
            bk.rtok = e1
            ev_t.append(e1)
        for half in range(2):
            bkp = pt[half]
            def mkT(half=half, bkp=bkp):
                def f(h):
                    last = None
                    for j in range(4):
                        tt = half * 4 + j
                        last = h.transpose(bkp.ap[:, j * 128:(j + 1) * 128],
                                           kvstrip[:, tt * 128:(tt + 1) * 128],
                                           idbf)
                    return last
                return f
            bkp.wtok = top(mkT(), after=ev_t + [bkp.rtok, t_idbf])
            def mkE(half=half, bkp=bkp):
                def f(h):
                    last = None
                    for j in range(4):
                        tt = half * 4 + j
                        last = h.tensor_copy(
                            vtstrip[:, tt * 128:(tt + 1) * 128],
                            bkp.ap[:, j * 128:(j + 1) * 128])
                    return last
                return f
            bkp.rtok = vop(mkE(), after=[bkp.wtok, vt_free[0]])
        vst = b.d("sync",
                  lambda h, vb=vb: [h.dma_start(
                      out=ccv_i[0, :].rearrange("(t d) -> t d", d=D)[
                          tt * 128:(tt + 1) * 128, vb * 128:(vb + 1) * 128],
                      in_=vtstrip[:, tt * 128:(tt + 1) * 128])
                      for tt in range(8)],
                  8, after=[pt[0].rtok, pt[1].rtok])
        v_stage.append(vst)
        vt_free[0] = vst
        strip_free[0] = pt[1].wtok
    t_ag_v = ag(ccv_i, ccv_o, v_stage + [t_ag_k])

    for blk in range(16):
        qkv_block(blk)
        ev_t = []
        for n in range(2):
            bk = pb[n]
            e1 = vop(lambda h, bk=bk, n=n, blk=blk: h.tensor_scalar_add(
                Qfm[:, blk * TOK + n * 512:blk * TOK + (n + 1) * 512],
                bk.ap, bq_sb[:, blk:blk + 1]),
                after=[bk.wtok, t_bq] + (x2_store_toks if blk == 0 else []))
            bk.rtok = e1
            ev_t.append(e1)
        sq = sop(lambda h, blk=blk: h.activation(
            sqstrip, Qfm[:, blk * TOK:(blk + 1) * TOK], AF.Square),
            after=ev_t + [strip_free[0]])
        def mkN(blk=blk):
            def f(h):
                h.matmul(pb[4].ap[0:1, :], onb, sqstrip[:, 0:512],
                         start=(blk == 0), stop=(blk == 15))
                return h.matmul(pb[5].ap[0:1, :], onb, sqstrip[:, 512:1024],
                                start=(blk == 0), stop=(blk == 15))
            return f
        qn_mm = top(mkN(), after=[sq, t_onb, pb[4].rtok, pb[5].rtok])
        strip_free[0] = qn_mm
        q_done += ev_t
    e_q1 = vop(lambda h: h.tensor_copy(qn_sb2[0:1, 0:512], pb[4].ap[0:1, :]),
               after=[qn_mm])
    e_q2 = vop(lambda h: h.tensor_copy(qn_sb2[0:1, 512:1024], pb[5].ap[0:1, :]),
               after=[qn_mm])
    pb[4].rtok = e_q1
    pb[5].rtok = e_q2
    e_q3 = sop(lambda h: h.activation(qn_sb2, qn_sb2, AF.Sqrt), after=[e_q1, e_q2])
    qnst = dma("sync", qn_d[0:1, :], qn_sb2, after=[e_q3])
    qnld = dma("scalar", qn_sb, qn_d[0, :].rearrange("(j p) -> p j", p=128),
               after=[qnst], slow=True)
    mk_ld2 = dma("scalar", mk_bf,
                 cc_k_out.rearrange("r c -> (r c)")[
                     D * TOK::KVC1].rearrange("(o n) -> o n", o=1),
                 after=[t_ag_k], slow=True)
    e_m1 = vop(lambda h: h.tensor_copy(srow[0:1, 0:8], mk_bf), after=[mk_ld2])
    e_m2 = vop(lambda h: h.reduce_max(sc1[0:1, 22:23], srow[0:1, 0:8], axis=AX),
               after=[e_m1])
    e_m3 = sop(lambda h: h.activation(sc1[0:1, 23:24], sc1[0:1, 22:23], AF.Sqrt),
               after=[e_m2])
    mkst = dma("sync", smallv[0:1, 23:24], sc1[0:1, 23:24], after=[e_m3])
    mkbc = dma("scalar", c128[:, 3:4], smallv[0:1, 23:24].broadcast_to((128, 1)),
               after=[mkst])
    t_bias = vop(lambda h: h.tensor_scalar(biasB, qn_sb, c128[:, 3:4],
                                           float(-ISQD), OP.mult, OP.mult),
                 after=[qnld, mkbc])

    t_lz = vop(lambda h: h.memset(lacc, 0.0))
    ctx_z = []
    last_qkv_mm = prev_wq[0] if prev_wq[0][1] > prev_wq[1][1] else prev_wq[1]
    for t in range(NT):
        ctx_z.append(vop(lambda h, t=t: h.memset(ctx[t], 0.0),
                         after=[last_qkv_mm] + q_done))
    prev_lp = [None] * NT
    pP_free = [None, None]
    ptq_free = [x2_store_toks[-1], x2_store_toks[-1]]
    kbuf_free = [None, None]
    vbuf_free = [None, None]
    kl_tok = [None, None]
    vl_tok = [None, None]
    sc_done = []
    pend = None

    def emit_ptpv(kb, q, ex):
        bkp = pt[q % 2]
        def mkPT(q=q):
            def f(h):
                h.transpose(pt[q % 2].ap[:, 0:128], pP[q % 2][:, 0:128], idbf)
                return h.transpose(pt[q % 2].ap[:, 128:256],
                                   pP[q % 2][:, 128:256], idbf)
            return f
        bkp.wtok = top(mkPT(), after=[ex, bkp.rtok, t_idbf])
        pe = vop(lambda h, q=q: h.tensor_copy(ptq[q % 2][:, 0:256],
                                              pt[q % 2].ap[:, 0:256]),
                 after=[bkp.wtok, ptq_free[q % 2]])
        bkp.rtok = pe
        pP_free[q % 2] = bkp.wtok
        pv_last = None
        for n in range(4):
            pvb = pb[2 + n % 2]
            def mkPV(q=q, n=n, kb=kb):
                def f(h):
                    last = None
                    for kc in range(2):
                        last = h.matmul(
                            pb[2 + n % 2].ap,
                            ptq[q % 2][:, kc * 128:(kc + 1) * 128],
                            vbuf[kb % 2].rearrange(
                                "p (tc d) -> p tc d", d=D)[:, kc,
                                                          n * 512:(n + 1) * 512],
                            start=(kc == 0), stop=(kc == 1))
                    return last
                return f
            pvb.wtok = top(mkPV(), after=[vl_tok[kb % 2], pe, pvb.rtok])
            ca = vop(lambda h, q=q, n=n, pvb=pvb: h.tensor_tensor(
                ctx[q][:, n * 512:(n + 1) * 512],
                ctx[q][:, n * 512:(n + 1) * 512], pvb.ap, OP.add),
                after=[pvb.wtok, ctx_z[q]])
            pvb.rtok = ca
            pv_last = ca
        ptq_free[q % 2] = pv_last
        return pv_last

    for kb in range(32):
        r, off = kb // 4, (kb % 4) * 256
        kl = dma("sync", kbuf[kb % 2].rearrange("p (c t) -> p c t", t=256),
                 cc_k_out[r, 0:D * TOK].rearrange(
                     "(c p t) -> p c t", p=128, t=TOK)[:, :, off:off + 256],
                 after=[t_ag_k, kbuf_free[kb % 2], qn_mm] + q_done)
        vl = dma("sync", vbuf[kb % 2].rearrange("p (tc d) -> p tc d", d=D),
                 ccv_o[r, :].rearrange(
                     "(tc p d) -> p tc d", p=128, d=D)[:, 2 * (kb % 4):2 * (kb % 4) + 2, :],
                 after=[t_ag_v, vbuf_free[kb % 2]])
        kl_tok[kb % 2] = kl
        vl_tok[kb % 2] = vl
        last_sc = None
        last_pv = None
        for q in range(NT):
            sbk = pb[q % 2]
            def mkS(kb=kb, q=q):
                def f(h):
                    last = None
                    for dc in range(DC):
                        last = h.matmul(
                            pb[q % 2].ap[:, 0:256],
                            Qfm[:, dc * TOK + q * 128:dc * TOK + (q + 1) * 128],
                            kbuf[kb % 2][:, dc * 256:(dc + 1) * 256],
                            start=(dc == 0), stop=(dc == DC - 1))
                    return last
                return f
            sbk.wtok = top(mkS(), after=[kl, sbk.rtok] + q_done)
            last_sc = sbk.wtok
            ex = sop(lambda h, q=q, kb=kb: h.activation(
                pP[q % 2][:, 0:256], pb[q % 2].ap[:, 0:256], AF.Exp,
                bias=biasB[:, q:q + 1], scale=float(ISQD),
                accum_out=lpart[:, q:q + 1]),
                after=[sbk.wtok, t_bias, prev_lp[q], pP_free[q % 2]])
            sbk.rtok = ex
            la = vop(lambda h, q=q: h.tensor_add(lacc[:, q:q + 1], lacc[:, q:q + 1],
                                                 lpart[:, q:q + 1]),
                     after=[ex, t_lz])
            prev_lp[q] = la
            if pend is not None:
                pkb, pq, pex = pend
                tokpv = emit_ptpv(pkb, pq, pex)
                if pq == NT - 1:
                    vbuf_free[pkb % 2] = tokpv
            pend = (kb, q, ex)
        kbuf_free[kb % 2] = last_sc
    pkb, pq, pex = pend
    last_pv = emit_ptpv(pkb, pq, pex)
    sc_done = [kbuf_free[0], kbuf_free[1], last_pv] + prev_lp
    t_lrec = vop(lambda h: h.reciprocal(lrec[:, 0:NT], lacc[:, 0:NT]),
                 after=[prev_lp[q] for q in range(NT)])

    ctxf = at("ctxf", OFF_A, (128, DC * TOK), BF16)
    ctxf_done = []
    for q in range(NT):
        for half in range(2):
            for qtr in range(2):
                bk = pb[4 + half]
                def mkT(q=q, half=half, qtr=qtr, bk=bk):
                    def f(h):
                        last = None
                        for j in range(4):
                            dc = half * 8 + qtr * 4 + j
                            last = h.transpose(bk.ap[:, j * 128:(j + 1) * 128],
                                               ctx[q][:, dc * 128:(dc + 1) * 128],
                                               id32)
                        return last
                    return f
                bk.wtok = top(mkT(), after=[bk.rtok, t_id32] + sc_done +
                              ([prev_lp[q]] if qtr == 0 and half == 0 else []))
                def mkE(q=q, half=half, qtr=qtr, bk=bk):
                    def f(h):
                        last = None
                        for j in range(4):
                            dc = half * 8 + qtr * 4 + j
                            last = h.tensor_copy(
                                ctxf[:, dc * TOK + q * 128:dc * TOK + (q + 1) * 128],
                                bk.ap[:, j * 128:(j + 1) * 128])
                        return last
                    return f
                bk.rtok = vop(mkE(), after=[bk.wtok])
                ctxf_done.append(bk.rtok)

    t_ob0 = bcast("scalar", bc[0], modL[1][0:1, 2 * D:3 * D],
                  after=[t_mod[1]] + sc_done)
    t_om = vop(lambda h: h.tensor_scalar(bc[2], bc[0], 0.0, None, OP.is_ge),
               after=[t_ob0] + sc_done)
    t_oa = vop(lambda h: h.tensor_scalar_max(wst[0], bc[0], 0.25),
               after=[t_ob0, t_om])
    t_obn = vop(lambda h: h.tensor_scalar_min(bc[0], bc[0], -0.25), after=[t_oa])
    t_oc1 = vop(lambda h: h.tensor_sub(wst[0], wst[0], bc[0]), after=[t_obn])
    t_oc2 = vop(lambda h: h.tensor_mul(wst[0], wst[0], bc[2]), after=[t_oc1])
    t_ga1 = vop(lambda h: h.tensor_add(bc[0], bc[0], wst[0]), after=[t_oc2])
    t_ob3 = bcast("scalar", bc[3], bout[0:1, :], after=sc_done + [t_gbe])

    x3_store_toks = []
    prev_wo = [sc_done[1], t_oc2]
    prev_x3 = {}
    prev_evo = [sc_done[-1]]
    for n in range(4):
        tws = []
        for hh in range(2):
            tw = b.d("gpsimd", lambda h, n=n, hh=hh: [h.dma_start(
                out=wobig[hh].rearrange("p (c q) -> p c q", q=512),
                in_=WoutT[hh * 1024:(hh + 1) * 1024,
                          n * 512:(n + 1) * 512].rearrange(
                    "(c p) q -> p c q", p=128))],
                1, after=[prev_wo[hh]])
            tws.append(tw)
        for tg in range(2):
            mm_group = []
            for dc in range(DC):
                hh = dc // 8
                def mkM(dc=dc, tg=tg, hh=hh):
                    def f(h):
                        last = None
                        for j in range(4):
                            t = tg * 4 + j
                            last = h.matmul(
                                pb[j].ap,
                                ctxf[:, dc * TOK + t * 128:dc * TOK + (t + 1) * 128],
                                wobig[hh][:, (dc % 8) * 512:(dc % 8 + 1) * 512],
                                start=(dc == 0), stop=(dc == DC - 1))
                        return last
                    return f
                mmt = top(mkM(), after=[tws[hh]] + ctxf_done +
                          [pb[j].rtok for j in range(4)])
                mm_group.append(mmt)
            prev_wo = [mm_group[-1], mm_group[-1]]
            for j in range(4):
                pb[j].wtok = mm_group[-1]
            for j in range(4):
                t = tg * 4 + j
                bkj = pb[j]
                lx = dma("scalar", tm0[:, j * 512:(j + 1) * 512],
                         xres[t * 128:(t + 1) * 128, n * 512:(n + 1) * 512],
                         after=[tm0_free[0], prev_x3.get((t, n), None)] +
                               x2_store_toks)
                e1 = vop(lambda h, bkj=bkj, n=n, t=t, j=j: h.scalar_tensor_tensor(
                    tm1[:, j * 512:(j + 1) * 512], bkj.ap, lrec[:, t:t + 1],
                    bc[3][:, n * 512:(n + 1) * 512], OP.mult, OP.add),
                    after=[bkj.wtok, t_ob3, t_lrec, prev_evo[0], tm1_free[0]])
                e2 = vop(lambda h, n=n, j=j: h.tensor_tensor(
                    tm1[:, j * 512:(j + 1) * 512], tm1[:, j * 512:(j + 1) * 512],
                    bc[0][:, n * 512:(n + 1) * 512], OP.mult),
                    after=[e1, t_ga1])
                e3 = vop(lambda h, j=j: h.tensor_add(
                    tm1[:, j * 512:(j + 1) * 512], tm1[:, j * 512:(j + 1) * 512],
                    tm0[:, j * 512:(j + 1) * 512]), after=[e2, lx])
                bkj.rtok = e3
                tm0_free[0] = e3
                st3 = dma("sync", xres[t * 128:(t + 1) * 128, n * 512:(n + 1) * 512],
                          tm1[:, j * 512:(j + 1) * 512], after=[e3])
                x3_store_toks.append(st3)
                prev_x3[(t, n)] = st3
                prev_evo[0] = st3
    tm1_free[:] = [prev_evo[0]]

    if nphase < 6:
        return _finalize(nc, b, taps, y_out)

    moe_gate = x3_store_toks + ctxf_done
    t_mb1 = bcast("scalar", bc[1], modL[1][0:1, 4 * D:5 * D],
                  after=[t_mod[1]] + moe_gate)
    t_mb1 = vop(lambda h: h.tensor_scalar_add(bc[1], bc[1], 1.0), after=[t_mb1])
    t_mb2 = bcast("scalar", bc[2], modL[1][0:1, 3 * D:4 * D],
                  after=[t_mod[1]] + moe_gate)

    h3_done = []
    rt_done = []
    logit_done = []
    for t in range(NT):
        tl = dma("sync", tm0, xres[t * 128:(t + 1) * 128, :],
                 after=[tm0_free[0]] + x3_store_toks)
        tm0_free[0] = tl
        cb2 = 32 * (t % 2)
        st_ = lambda c, cb2=cb2: statw[:, cb2 + c:cb2 + c + 1]
        guard = list(statg[t % 2])
        statg[t % 2] = []
        t1 = vop(lambda h, st_=st_: h.reduce_sum(st_(1), tm0, axis=AX),
                 after=[tl] + guard)
        t2 = sop(lambda h: h.activation(tm1, tm0, AF.Square), after=[tl] + tm1_free)
        t3 = vop(lambda h, st_=st_: h.reduce_sum(st_(2), tm1, axis=AX), after=[t2])
        t4 = vop(lambda h, st_=st_: h.tensor_scalar_mul(st_(1), st_(1),
                                                        float(1.0 / D)), after=[t1])
        t5 = vop(lambda h, st_=st_: h.tensor_mul(st_(3), st_(1), st_(1)), after=[t4])
        t6 = vop(lambda h, st_=st_: h.tensor_scalar_mul(st_(2), st_(2),
                                                        float(1.0 / D)), after=[t3])
        t7 = vop(lambda h, st_=st_: h.tensor_sub(st_(2), st_(2), st_(3)),
                 after=[t5, t6])
        t8 = sop(lambda h, st_=st_: h.activation(st_(4), st_(2), AF.Sqrt,
                                                 bias=cvec[:, 0:1]),
                 after=[t7, t_cvec])
        t9 = vop(lambda h, st_=st_: h.reciprocal(st_(5), st_(4)), after=[t8])
        ta = vop(lambda h, st_=st_: h.tensor_scalar(tm1, tm0, st_(1), st_(5),
                                                    OP.subtract, OP.mult),
                 after=[t9, t3])
        tb = vop(lambda h: h.tensor_tensor(tm1, tm1, bc[1], OP.mult),
                 after=[ta, t_mb1])
        tc = vop(lambda h: h.tensor_tensor(tm1, tm1, bc[2], OP.add),
                 after=[tb, t_mb2])
        statg[t % 2].append(t8)
        tm0_free[0] = ta
        lg_mm = None
        for dc in range(DC):
            bk = pb[4 + dc % 2]
            bk.wtok = top(lambda h, bk=bk, dc=dc: h.transpose(
                bk.ap[:, 0:128], tm1[:, dc * 128:(dc + 1) * 128], id32),
                after=[tc, bk.rtok, t_id32])
            ef = vop(lambda h, bk=bk: h.tensor_copy(hfm32, bk.ap[:, 0:128]),
                     after=[bk.wtok, lg_mm])
            eb = vop(lambda h, bk=bk, dc=dc, t=t: h.tensor_copy(
                h3fm[:, dc * TOK + t * 128:dc * TOK + (t + 1) * 128],
                bk.ap[:, 0:128]),
                after=[bk.wtok] + (moe_gate if t == 0 and dc == 0 else []))
            bk.rtok = eb
            h3_done.append(eb)
            lg_mm = top(lambda h, dc=dc: h.matmul(
                pb[2].ap[:, 0:8], hfm32, gateTs[:, dc * 8:(dc + 1) * 8],
                start=(dc == 0), stop=(dc == DC - 1)),
                after=[ef, t_gateT, pb[2].rtok])
        pb[2].wtok = lg_mm
        r0 = vop(lambda h: h.tensor_copy(rt[:, 0:8], pb[2].ap[:, 0:8]),
                 after=[lg_mm] + rt_done)
        pb[2].rtok = r0
        r1 = vop(lambda h: h.reduce_max(rt[:, 8:9], rt[:, 0:8], axis=AX), after=[r0])
        r2 = vop(lambda h: h.tensor_scalar_mul(rt[:, 8:9], rt[:, 8:9], -1.0),
                 after=[r1])
        r3 = sop(lambda h: h.activation(rt[:, 0:8], rt[:, 0:8], AF.Exp,
                                        bias=rt[:, 8:9]), after=[r2])
        r4 = vop(lambda h: h.reduce_sum(rt[:, 9:10], rt[:, 0:8], axis=AX),
                 after=[r3])
        r5 = vop(lambda h: h.reciprocal(rt[:, 10:11], rt[:, 9:10]), after=[r4])
        r6 = vop(lambda h: h.tensor_scalar(rt[:, 0:8], rt[:, 0:8], rt[:, 10:11],
                                           None, OP.mult), after=[r5])
        g1_ = vop(lambda h: h.tensor_tensor(rt2[:, 0:4], rt[:, 0:8:2],
                                            rt[:, 1:8:2], OP.max), after=[r6])
        g2_ = vop(lambda h: h.reduce_max(rt2[:, 4:5], rt2[:, 0:4], axis=AX),
                  after=[g1_])
        g3_ = vop(lambda h: h.tensor_scalar(rt2[:, 5:9], rt2[:, 0:4], rt2[:, 4:5],
                                            None, OP.is_ge), after=[g2_])
        g4_ = vop(lambda h: h.tensor_scalar(rt2[:, 5:9], rt2[:, 5:9], -1.0, -1.0,
                                            OP.mult, OP.subtract), after=[g3_])
        g5_ = vop(lambda h: h.tensor_tensor(rt2[:, 5:9], rt2[:, 0:4], rt2[:, 5:9],
                                            OP.mult), after=[g4_])
        g6_ = vop(lambda h: h.reduce_max(rt2[:, 9:10], rt2[:, 5:9], axis=AX),
                  after=[g5_])
        g7_ = vop(lambda h: h.tensor_scalar(rt2[:, 10:14], rt2[:, 0:4], rt2[:, 9:10],
                                            None, OP.is_ge), after=[g6_])
        m1_ = vop(lambda h: h.tensor_tensor(rt[:, 0:8:2], rt[:, 0:8:2],
                                            rt2[:, 10:14], OP.mult), after=[g7_])
        m2_ = vop(lambda h: h.tensor_tensor(rt[:, 1:8:2], rt[:, 1:8:2],
                                            rt2[:, 10:14], OP.mult), after=[m1_])
        n1 = vop(lambda h: h.reduce_max(rt[:, 8:9], rt[:, 0:8], axis=AX),
                 after=[m2_])
        n2 = vop(lambda h: h.tensor_scalar(rt[:, 10:11] if False else rt2[:, 0:8],
                                           rt[:, 0:8], rt[:, 8:9], None, OP.is_ge),
                 after=[n1])
        n3 = vop(lambda h: h.tensor_scalar(rt2[:, 0:8], rt2[:, 0:8], -1.0, -1.0,
                                           OP.mult, OP.subtract), after=[n2])
        n4 = vop(lambda h: h.tensor_tensor(rt2[:, 0:8], rt[:, 0:8], rt2[:, 0:8],
                                           OP.mult), after=[n3])
        n5 = vop(lambda h: h.reduce_max(rt[:, 9:10], rt2[:, 0:8], axis=AX),
                 after=[n4])
        n6 = vop(lambda h: h.tensor_scalar(rt2[:, 0:8], rt[:, 0:8], rt[:, 9:10],
                                           None, OP.is_ge), after=[n5])
        n7 = vop(lambda h, t=t: h.tensor_tensor(wmat_sb[:, t * 8:(t + 1) * 8],
                                                rt[:, 0:8], rt2[:, 0:8], OP.mult),
                 after=[n6])
        rt_done = [n7]
        logit_done.append(n7)

    acc_z = []
    for t in range(NT):
        acc_z.append(vop(lambda h, t=t: h.memset(acc[t], 0.0),
                         after=ctxf_done + [prev_wo[0], prev_wo[1]] + x3_store_toks))

    gu_free = [sc_done[-1]]
    prev_gw = [h3_done[-1], h3_done[-1]]
    prev_dw = [h3_done[-1], h3_done[-1]]
    acc_done = list(acc_z)
    silu_strip_free = [sc_done[-1], sc_done[-1]]
    for ee in range(10):
        is_sh = ee >= 8
        shh = ee - 8
        NF = 11
        gu_toks = []
        for fb in range(NF):
            bi = fb % 2
            if is_sh:
                gsrc = sgWT[:, (shh * 11 + fb) * 128:(shh * 11 + fb + 1) * 128]
                usrc = suWT[:, (shh * 11 + fb) * 128:(shh * 11 + fb + 1) * 128]
            else:
                gsrc = gWT[ee, :, fb * 128:(fb + 1) * 128]
                usrc = uWT[ee, :, fb * 128:(fb + 1) * 128]
            twg = b.d("gpsimd", lambda h, gsrc=gsrc, bi=bi: [h.dma_start(
                out=gwb[bi].rearrange("p (c q) -> p c q", q=128),
                in_=gsrc.rearrange("(c p) q -> p c q", p=128))],
                1, after=[prev_gw[bi]])
            twu = b.d("gpsimd", lambda h, usrc=usrc, bi=bi: [h.dma_start(
                out=uwb[bi].rearrange("p (c q) -> p c q", q=128),
                in_=usrc.rearrange("(c p) q -> p c q", p=128))],
                1, after=[prev_gw[bi]])
            mms = []
            for n in range(2):
                gbk, ubk = pb[0 + n], pb[2 + n]
                def mkG(bi=bi, n=n):
                    def f(h):
                        last = None
                        for dc in range(DC):
                            last = h.matmul(
                                pb[0 + n].ap, gwb[bi][:, dc * 128:(dc + 1) * 128],
                                h3fm[:, dc * TOK + n * 512:dc * TOK + (n + 1) * 512],
                                start=(dc == 0), stop=(dc == DC - 1))
                        return last
                    return f
                def mkU(bi=bi, n=n):
                    def f(h):
                        last = None
                        for dc in range(DC):
                            last = h.matmul(
                                pb[2 + n].ap, uwb[bi][:, dc * 128:(dc + 1) * 128],
                                h3fm[:, dc * TOK + n * 512:dc * TOK + (n + 1) * 512],
                                start=(dc == 0), stop=(dc == DC - 1))
                        return last
                    return f
                gbk.wtok = top(mkG(), after=[twg, gbk.rtok] + h3_done)
                ubk.wtok = top(mkU(), after=[twu, ubk.rtok] + h3_done)
                mms += [gbk.wtok, ubk.wtok]
            prev_gw[bi] = mms[-1]
            for n in range(2):
                gbk, ubk = pb[0 + n], pb[2 + n]
                s1_ = sop(lambda h, gbk=gbk, n=n: h.activation(
                    wobuf[n], gbk.ap, AF.Silu),
                    after=[gbk.wtok, silu_strip_free[n]])
                gbk.rtok = s1_
                gu1 = vop(lambda h, ubk=ubk, fb=fb, n=n: h.tensor_tensor(
                    gu_e[:, fb * TOK + n * 512:fb * TOK + (n + 1) * 512],
                    wobuf[n], ubk.ap, OP.mult),
                    after=[s1_, ubk.wtok, gu_free[0]])
                ubk.rtok = gu1
                gu_toks.append(gu1)
                silu_strip_free[n] = gu1
        ph2 = []
        for n in range(4):
            bi = n % 2
            dsrc = (sdWT[shh * 11 * 128:(shh + 1) * 11 * 128,
                         n * 512:(n + 1) * 512] if is_sh else
                    dWT[ee, :, n * 512:(n + 1) * 512])
            twd = b.d("gpsimd", lambda h, dsrc=dsrc, bi=bi: [h.dma_start(
                out=dwn[bi].rearrange("p (c q) -> p c q", q=512),
                in_=dsrc.rearrange("(c p) q -> p c q", p=128))],
                1, after=[prev_dw[bi]])
            for tg in range(2):
                mmds = []
                for fb in range(NF):
                    def mkD(fb=fb, tg=tg, bi=bi):
                        def f(h):
                            last = None
                            for j in range(4):
                                t = tg * 4 + j
                                last = h.matmul(
                                    pb[j].ap,
                                    gu_e[:, fb * TOK + t * 128:fb * TOK + (t + 1) * 128],
                                    dwn[bi][:, fb * 512:(fb + 1) * 512],
                                    start=(fb == 0), stop=(fb == NF - 1))
                            return last
                        return f
                    mmd = top(mkD(), after=[twd] + gu_toks +
                              [pb[j].rtok for j in range(4)])
                    mmds.append(mmd)
                prev_dw[bi] = mmds[-1]
                for j in range(4):
                    pb[j].wtok = mmds[-1]
                for j in range(4):
                    t = tg * 4 + j
                    bkj = pb[j]
                    if is_sh:
                        ea = vop(lambda h, bkj=bkj, t=t, n=n: h.tensor_tensor(
                            acc[t][:, n * 512:(n + 1) * 512],
                            acc[t][:, n * 512:(n + 1) * 512], bkj.ap, OP.add),
                            after=[bkj.wtok] + acc_done)
                    else:
                        ea = vop(lambda h, bkj=bkj, t=t, n=n, ee=ee:
                                 h.scalar_tensor_tensor(
                                     acc[t][:, n * 512:(n + 1) * 512], bkj.ap,
                                     wmat_sb[:, t * 8 + ee:t * 8 + ee + 1],
                                     acc[t][:, n * 512:(n + 1) * 512],
                                     OP.mult, OP.add),
                                 after=[bkj.wtok] + acc_done + logit_done)
                    bkj.rtok = ea
                    ph2.append(ea)
        acc_done = ph2[-8:]
        gu_free[0] = ph2[-1]

    t_yb0 = bcast("scalar", bc[0], modL[1][0:1, 5 * D:6 * D],
                  after=[t_mod[1], prev_dw[0], prev_dw[1]] + acc_done)
    t_ym = vop(lambda h: h.tensor_scalar(bc[2], bc[0], 0.0, None, OP.is_ge),
               after=[t_yb0] + acc_done)
    t_ya = vop(lambda h: h.tensor_scalar_max(wst[0], bc[0], 0.25),
               after=[t_yb0, t_ym])
    t_ybn = vop(lambda h: h.tensor_scalar_min(bc[0], bc[0], -0.25), after=[t_ya])
    t_yc1 = vop(lambda h: h.tensor_sub(wst[0], wst[0], bc[0]), after=[t_ybn])
    t_yc2 = vop(lambda h: h.tensor_mul(wst[0], wst[0], bc[2]), after=[t_yc1])
    t_gb1 = vop(lambda h: h.tensor_add(bc[0], bc[0], wst[0]), after=[t_yc2])
    y_toks = []
    prev_y = [None]
    tm0_free[0] = acc_done[-1]
    tm1_free[:] = [acc_done[-1]]
    for t in range(NT):
        lx = dma("scalar", tm0, xres[t * 128:(t + 1) * 128, :],
                 after=[tm0_free[0]] + x3_store_toks)
        y1 = vop(lambda h, t=t: h.tensor_tensor(tm1, acc[t], bc[0], OP.mult),
                 after=[t_gb1] + acc_done + tm1_free + [prev_y[0]])
        y2 = vop(lambda h: h.tensor_add(tm1, tm1, tm0), after=[y1, lx])
        tm0_free[0] = y2
        tm1_free[:] = [y2]
        yst = dma("sync", y_out[t * 128:(t + 1) * 128, :], tm1, after=[y2])
        y_toks.append(yst)
        prev_y[0] = yst

    return _finalize(nc, b, taps, y_out)


def _finalize(nc, b, taps, y_out):
    allt = [(s, v) for s, v in b.cnt.items()]
    tap_toks = []
    for name, (src, dst) in taps.items():
        tt = b.d("sync", lambda h, s=src, d=dst: [h.dma_start(out=d, in_=s)],
                 1, after=list(allt))
        tap_toks.append(tt)
    fin = [(s, v) for s, v in b.cnt.items()]
    b.c("sync", lambda h: None, after=fin)
    _emit(nc, b)
    return nc


def _prep_inputs(inputs):
    f32 = np.float32
    x = np.ascontiguousarray(np.asarray(inputs["L0_attn_hidden_states"],
                                        dtype=f32).reshape(8192, 2048))
    c01 = np.concatenate([np.asarray(inputs["L0_c"], f32).reshape(1, 2048),
                          np.asarray(inputs["L1_c"], f32).reshape(1, 2048)], axis=1)
    wmod0_full = np.asarray(inputs["L0_wmod"], f32)
    wmod1_full = np.asarray(inputs["L1_wmod"], f32)

    common = dict(
        WiT=np.ascontiguousarray(np.asarray(inputs["L0_attn_Wi"], f32).T),
        WfT=np.ascontiguousarray(np.asarray(inputs["L0_attn_Wf"], f32).T),
        WgT=np.ascontiguousarray(np.asarray(inputs["L0_attn_Wg"], f32).T),
        WoT=np.ascontiguousarray(np.asarray(inputs["L0_attn_Wo"], f32).T),
        W8T=np.ascontiguousarray(np.asarray(inputs["L0_ffn_weight"]).T),
        nrm_i=np.asarray(inputs["L0_attn_norm_i"], f32).reshape(1, 2048),
        nrm_f=np.asarray(inputs["L0_attn_norm_f"], f32).reshape(1, 2048),
        nrm_g=np.asarray(inputs["L0_attn_norm_g"], f32).reshape(1, 2048),
        nrm_o=np.asarray(inputs["L0_attn_norm_o"], f32).reshape(1, 2048),
        gnw=np.asarray(inputs["L0_attn_g_norm_w"], f32).reshape(1, 2048),
        w8s=np.asarray(inputs["L0_ffn_weight_scale"], f32).reshape(1, 1),
        in_s=np.asarray(inputs["L0_ffn_input_scale"]).reshape(1, 1),
        c01=c01,
        WqkvT=np.ascontiguousarray(np.asarray(inputs["L1_attn_w_qkv"], f32).T),
        bqkv=np.asarray(inputs["L1_attn_b_qkv"], f32).reshape(1, 6144),
        WoutT=np.ascontiguousarray(np.asarray(inputs["L1_attn_w_out"], f32).T),
        bout=np.asarray(inputs["L1_attn_b_out"], f32).reshape(1, 2048),
        gateT=np.ascontiguousarray(np.asarray(inputs["L1_ffn_gate_weight"], f32).T),
        gWT=np.ascontiguousarray(
            np.asarray(inputs["L1_ffn_gate_w"], f32).transpose(0, 2, 1)),
        uWT=np.ascontiguousarray(
            np.asarray(inputs["L1_ffn_up_w"], f32).transpose(0, 2, 1)),
        dWT=np.ascontiguousarray(
            np.asarray(inputs["L1_ffn_down_w"], f32).transpose(0, 2, 1)),
        sgWT=np.ascontiguousarray(np.asarray(inputs["L1_ffn_sh_gate_w"], f32).T),
        suWT=np.ascontiguousarray(np.asarray(inputs["L1_ffn_sh_up_w"], f32).T),
        sdWT=np.ascontiguousarray(np.asarray(inputs["L1_ffn_sh_down_w"], f32).T),
        id32=np.eye(128, dtype=f32),
        cvec=np.array([[1e-5, 1e-8, 0, 0, 0, 0, 0, 0]], dtype=f32),
        idbf=np.eye(128, dtype=ml_dtypes.bfloat16),
    )
    in_maps = []
    for r in range(NCORE):
        m = dict(common)
        m["x"] = np.ascontiguousarray(x[r * TOK:(r + 1) * TOK])
        m["wmod0"] = np.ascontiguousarray(wmod0_full[:, r * 1536:(r + 1) * 1536])
        m["wmod1"] = np.ascontiguousarray(wmod1_full[:, r * 1536:(r + 1) * 1536])
        rm = np.zeros((1, 8), f32)
        rm[0, :r] = 1.0
        m["rmask"] = rm
        in_maps.append(m)
    return in_maps


def kernel(**inputs):
    in_maps = _prep_inputs(inputs)
    nc = build_nc()
    res = run_bass_kernel_spmd(nc, in_maps, core_ids=list(range(NCORE)))
    y = np.concatenate([np.asarray(r["y"]) for r in res.results], axis=0)
    return y.reshape(1, 8192, 2048).astype(np.float32)
